# version 12
# baseline (speedup 1.0000x reference)
import contextlib
import math

import numpy as np

import concourse.bass as bass
import concourse.mybir as mybir
from concourse.bass_utils import run_bass_kernel_spmd

F32 = mybir.dt.float32
BF16 = mybir.dt.bfloat16
AF = mybir.ActivationFunctionType
ALU = mybir.AluOpType
AX = mybir.AxisListType

S, D, H, DK, DV, SEG, NSEG = 8192, 2048, 16, 128, 128, 512, 16
NC = 8
HPC = H // NC
ROWS = S // NC
NCH = D // 128
SCALE = 1.0 / math.sqrt(DK)
EPS = 1e-5
NSUP = 8
SUPI = 8192 // NSUP

EXPB = -20.0


class _Res:
    __slots__ = ("w", "r", "strict", "name")

    def __init__(self, name="", strict=False):
        self.w = {}
        self.r = {}
        self.strict = strict
        self.name = name


class _Op:
    __slots__ = ("eng", "dom", "dma", "builder", "ord", "deps", "clock",
                 "needs_inc", "val", "waits", "tag")


class _Plan:
    ENGS = ("sync", "gpsimd", "vector", "scalar", "tensor")

    def __init__(self):
        self.eng_ops = {e: [] for e in self.ENGS}
        self.dom_ord = {}
        self.dom_last = {}
        self.all = []

    def op(self, eng, builder, reads=(), writes=(), dma=False, tag="",
           after=()):
        o = _Op()
        o.eng, o.dma, o.builder, o.tag = eng, dma, builder, tag
        o.dom = ("D" + eng) if dma else eng
        o.deps = {}
        o.needs_inc = False
        o.ord = self.dom_ord.get(o.dom, 0) + 1
        self.dom_ord[o.dom] = o.ord

        def dep(p, force):
            if p is None or p is o:
                return
            if (not force) and p.dom == o.dom and (not p.dma) and (not o.dma):
                return
            cur = o.deps.get(p.dom)
            if cur is None or p.ord > cur.ord:
                o.deps[p.dom] = p

        for p in after:
            dep(p, True)
        for res in reads:
            for p in res.w.values():
                dep(p, res.strict)
        for res in writes:
            for p in res.w.values():
                dep(p, res.strict)
            for p in res.r.values():
                dep(p, res.strict or p.dma or o.dma)
        for res in reads:
            res.r[o.dom] = o
        for res in writes:
            res.w = {o.dom: o}
            res.r = {}
        self.eng_ops[eng].append(o)
        self.all.append(o)
        self.dom_last[o.dom] = o
        return o

    def fence(self):
        tails = list(self.dom_last.values())
        for eng in self.ENGS:
            self.op(eng, lambda e: e.nop(), after=tails, tag="fence")

    def resolve(self):
        last_clock = {}
        for o in self.all:
            ck = dict(last_clock.get(o.eng, ()))
            o.waits = []
            for dom in sorted(o.deps):
                p = o.deps[dom]
                if ck.get(dom, 0) >= p.ord:
                    continue
                o.waits.append(p)
                p.needs_inc = True
                for d2, v2 in p.clock.items():
                    if ck.get(d2, 0) < v2:
                        ck[d2] = v2
                if ck.get(dom, 0) < p.ord:
                    ck[dom] = p.ord
            o.clock = ck
            last_clock[o.eng] = ck
        cnt = {}
        for o in self.all:
            if o.dma:
                o.needs_inc = True
            if o.needs_inc:
                v = cnt.get(o.dom, 0) + (16 if o.dma else 1)
                cnt[o.dom] = v
                o.val = v

    def emit(self, nc):
        self.resolve()
        doms = sorted({o.dom for o in self.all if o.needs_inc}
                      | {o.dom for o in self.all for p in [None]}
                      | {p.dom for o in self.all for p in o.waits})
        with contextlib.ExitStack() as es:
            sems = {d: es.enter_context(nc.semaphore(f"sem_{d}")) for d in doms}
            block = es.enter_context(nc.Block())

            def section(engname):
                ops = self.eng_ops[engname]

                def run(e):
                    for o in ops:
                        for p in o.waits:
                            e.wait_ge(sems[p.dom], p.val)
                        ins = o.builder(e)
                        if o.tag:
                            ins.annotate(o.tag)
                        if o.needs_inc:
                            ins.then_inc(sems[o.dom], 16 if o.dma else 1)
                return run

            for engname in self.ENGS:
                if self.eng_ops[engname]:
                    getattr(block, engname)(section(engname))


def build_program(nseg=NSEG, nsup=NSUP, scr_out=False):
    nc = bass.Bass("TRN2", target_bir_lowering=False)

    xT = nc.dram_tensor("xT", [D, S], F32, kind="ExternalInput").ap()
    xrt = nc.dram_tensor("xrt", [D, ROWS], F32, kind="ExternalInput").ap()
    wq = nc.dram_tensor("wq", [D, HPC * DK], F32, kind="ExternalInput").ap()
    wk = nc.dram_tensor("wk", [D, HPC * DK], F32, kind="ExternalInput").ap()
    wv = nc.dram_tensor("wv", [D, HPC * DV], F32, kind="ExternalInput").ap()
    wo = nc.dram_tensor("wo", [D, D], F32, kind="ExternalInput").ap()
    wgut = nc.dram_tensor("wgut", [D, 2 * 8192], F32, kind="ExternalInput").ap()
    wdt = nc.dram_tensor("wdt", [8192, D], F32, kind="ExternalInput").ap()
    betasT = nc.dram_tensor("betasT", [128, HPC], F32, kind="ExternalInput").ap()
    outT = nc.dram_tensor("outT", [D, ROWS], F32, kind="ExternalOutput").ap()

    scr_kind = "ExternalOutput" if scr_out else "Internal"
    att_scr = nc.dram_tensor("att_scr", [D, ROWS], F32, kind=scr_kind).ap()

    import ml_dtypes
    tri_np = np.triu(np.ones((128, 128), np.float32)).astype(ml_dtypes.bfloat16)
    tri_dram = nc.inline_tensor(tri_np, "tri_c").ap()

    P = _Plan()
    R = lambda name, strict=False: _Res(name, strict)

    es = contextlib.ExitStack()
    esa = contextlib.ExitStack()
    sb = lambda name, shape, dt: es.enter_context(nc.sbuf_tensor(name, shape, dt))
    sba = lambda name, shape, dt: esa.enter_context(nc.sbuf_tensor(name, shape, dt))
    ps = lambda name, shape, dt=F32: es.enter_context(nc.psum_tensor(name, shape, dt))

    ident = sb("ident", [128, 128], BF16)
    ones_c = sb("ones_c", [128, 1], BF16)
    ones_r = sb("ones_r", [1, 128], BF16)
    epsc = sb("epsc", [1, 1], F32)
    identf = sba("identf", [128, 128], F32)
    ones_rf = sba("ones_rf", [1, 128], F32)
    trimask = sba("trimask", [128, 128], BF16)
    negC = sba("negC", [128, 1], F32)
    beta = sba("beta", [128, HPC], F32)
    browf = sba("browf", [1, 2 * 128], F32)
    b1rowf = sba("b1rowf", [1, 2 * 128], F32)
    wq_b = sba("wq_b", [128, NCH * HPC * DK], BF16)
    wk_b = sba("wk_b", [128, NCH * HPC * DK], BF16)
    wv_b = sba("wv_b", [128, NCH * HPC * DV], BF16)
    wo_b = sba("wo_b", [128, NCH * D], BF16)
    wsum = sba("wsum", [1, 3 * HPC * DK], BF16)
    memz_f = [sba(f"memz_f{h}", [128, DK + 1], F32) for h in range(HPC)]
    memz_b = [sba(f"memz_b{h}", [128, DK + 1], BF16) for h in range(HPC)]
    betaB = [sba(f"betaB{h}", [128, DK + 1], F32) for h in range(HPC)]
    b1mB4 = [sba(f"b1mB4{h}", [128, 512], BF16) for h in range(HPC)]
    memb_beta = [sba(f"membeta{h}", [128, DK + 1], BF16) for h in range(HPC)]

    r_ident = R("ident")
    r_identf = R("identf")
    r_epsc = R("epsc", True)
    r_negC = R("negC", True)
    r_ones_c, r_ones_r, r_ones_rf = R("ones_c"), R("ones_r"), R("ones_rf")
    r_trimask = R("trimask")
    r_beta = R("beta", True)
    r_browf, r_b1rowf = R("browf", True), R("b1rowf", True)
    r_wq, r_wk, r_wv, r_wo = R("wq_b"), R("wk_b"), R("wv_b"), R("wo_b")
    r_wsum = R("wsum", True)
    r_memz_f = [R(f"memz_f{h}") for h in range(HPC)]
    r_memz_b = [R(f"memz_b{h}") for h in range(HPC)]
    r_betaB = [R(f"betaB{h}") for h in range(HPC)]
    r_b1mB4 = [R(f"b1mB4{h}") for h in range(HPC)]
    r_membeta = [R(f"membeta{h}") for h in range(HPC)]
    r_xT = R("xT")
    r_scr = R("att_scr")
    r_outT = R("outT")

    p0 = ps("p0", [128, 512])
    p1 = ps("p1", [128, 512])
    p2 = ps("p2", [128, 512])
    p3 = ps("p3", [128, 512])
    p4 = ps("p4", [128, 512])
    p5 = ps("p5", [128, 512])
    p6 = ps("p6", [128, 512])
    p7 = ps("p7", [128, 512])
    p7b = p7[:, :].bitcast(BF16)
    psums = [p0, p1, p2, p3, p4, p5, p6, p7]
    rp = [R(f"p{i}") for i in range(8)]
    r_p7s = rp[7]
    r_p7t = [rp[7] for _ in range(HPC)]

    P.op("gpsimd", lambda e: e.memset(ident[:, :], 0.0), writes=[r_ident])
    P.op("gpsimd", lambda e: e.affine_select(
        out=ident[:, :], in_=ident[:, :], compare_op=ALU.not_equal, fill=1.0,
        base=0, pattern=[[-1, 128]], channel_multiplier=1),
        reads=[r_ident], writes=[r_ident])
    P.op("gpsimd", lambda e: e.memset(identf[:, :], 0.0), writes=[r_identf])
    P.op("gpsimd", lambda e: e.affine_select(
        out=identf[:, :], in_=identf[:, :], compare_op=ALU.not_equal, fill=1.0,
        base=0, pattern=[[-1, 128]], channel_multiplier=1),
        reads=[r_identf], writes=[r_identf])
    P.op("gpsimd", lambda e: e.memset(ones_c[:, :], 1.0), writes=[r_ones_c])
    P.op("gpsimd", lambda e: e.memset(ones_r[:, :], 1.0), writes=[r_ones_r])
    P.op("gpsimd", lambda e: e.memset(ones_rf[:, :], 1.0), writes=[r_ones_rf])
    P.op("gpsimd", lambda e: e.memset(epsc[:, :], EPS), writes=[r_epsc])
    P.op("gpsimd", lambda e: e.memset(negC[:, :], EXPB), writes=[r_negC])
    P.op("sync", lambda e: e.dma_start(out=trimask[:, :], in_=tri_dram[:, :]),
         writes=[r_trimask], dma=True)

    P.op("sync", lambda e: e.dma_start(out=beta[:, :], in_=betasT[:, :]),
         writes=[r_beta], dma=True)
    P.op("scalar", lambda e: e.activation(beta[:, :], beta[:, :], AF.Sigmoid),
         reads=[r_beta], writes=[r_beta])
    for h in range(HPC):
        P.op("tensor", (lambda h=h: lambda e: e.matmul(
            p7[0:1, 256:384], beta[:, h:h + 1], identf[:, :],
            start=True, stop=True))(),
            reads=[r_beta, r_identf], writes=[r_p7s], tag="brow_mm")
        P.op("vector", (lambda h=h: lambda e: e.tensor_copy(
            browf[0:1, h * 128:(h + 1) * 128], p7[0:1, 256:384]))(),
            reads=[r_p7s], writes=[r_browf], tag="brow_ev")
    P.op("vector", lambda e: e.tensor_scalar(b1rowf[0:1, :], browf[0:1, :],
                                             -1.0, 1.0, ALU.mult, ALU.add),
         reads=[r_browf], writes=[r_b1rowf])
    for h in range(HPC):
        P.op("tensor", (lambda h=h: lambda e: e.matmul(
            p2[:, 0:128], ones_rf[:, :], browf[0:1, h * 128:(h + 1) * 128],
            start=True, stop=True))(),
            reads=[r_ones_rf, r_browf], writes=[rp[2]], tag="betabc")
        P.op("vector", (lambda h=h: lambda e: e.tensor_copy(
            betaB[h][:, 0:128], p2[:, 0:128]))(),
            reads=[rp[2]], writes=[r_betaB[h]], tag="betabc_ev")
        P.op("vector", (lambda h=h: lambda e: e.memset(
            betaB[h][:, 128:129], 1.0))(),
            reads=[r_betaB[h]], writes=[r_betaB[h]])
        P.op("tensor", (lambda h=h: lambda e: e.matmul(
            p3[:, 0:128], ones_rf[:, :], b1rowf[0:1, h * 128:(h + 1) * 128],
            start=True, stop=True))(),
            reads=[r_ones_rf, r_b1rowf], writes=[rp[3]], tag="b1mbc")
        for t in range(4):
            P.op("vector", (lambda h=h, t=t: lambda e: e.tensor_copy(
                b1mB4[h][:, t * 128:(t + 1) * 128], p3[:, 0:128]))(),
                reads=[rp[3]], writes=[r_b1mB4[h]], tag="b1mbc_ev")

    def cast_load(dst, dstv, src, res, n3):
        P.op("gpsimd", lambda e: e.dma_start(out=dstv, in_=src), writes=[res],
             dma=True, tag="wload")

    wq_v = wq_b[:, :].rearrange("p (c m) -> p c m", c=NCH)
    wk_v = wk_b[:, :].rearrange("p (c m) -> p c m", c=NCH)
    wv_v = wv_b[:, :].rearrange("p (c m) -> p c m", c=NCH)
    cast_load(wq_b, wq_v, wq.rearrange("(c p) m -> p c m", p=128), r_wq, NCH)
    cast_load(wk_b, wk_v, wk.rearrange("(c p) m -> p c m", p=128), r_wk, NCH)
    cast_load(wv_b, wv_v, wv.rearrange("(c p) m -> p c m", p=128), r_wv, NCH)
    wo_v = wo_b[:, :].rearrange("p (c m) -> p c m", c=NCH)

    def wo_load():
        P.op("gpsimd", lambda e: e.dma_start(
            out=wo_v, in_=wo.rearrange("(c p) m -> p c m", p=128)),
            writes=[r_wo], dma=True, tag="wo_load")

    for wi, (wbuf, res) in enumerate(((wq_b, r_wq), (wk_b, r_wk), (wv_b, r_wv))):
        W = HPC * DK
        for c in range(NCH):
            P.op("tensor", (lambda wbuf=wbuf, c=c, W=W: lambda e: e.matmul(
                p0[0:1, 0:W], ones_c[:, :], wbuf[:, c * W:(c + 1) * W],
                start=(c == 0), stop=(c == NCH - 1)))(),
                reads=[res, r_ones_c], writes=[rp[0]], tag="wsum_mm")
        P.op("scalar", (lambda wi=wi, W=W: lambda e: e.mul(
            wsum[0:1, wi * W:(wi + 1) * W], p0[0:1, 0:W], -1.0))(),
            reads=[rp[0]], writes=[r_wsum], tag="wsum_evac")

    for h in range(HPC):
        P.op("vector", (lambda h=h: lambda e: e.memset(memz_f[h][:, 0:DK], 0.0))(),
             writes=[r_memz_f[h]])
        P.op("vector", (lambda h=h: lambda e: e.memset(
            memz_f[h][:, DK:DK + 1], 1.0 / DK))(),
            reads=[r_memz_f[h]], writes=[r_memz_f[h]])
        P.op("scalar", (lambda h=h: lambda e: e.copy(
            memz_b[h][:, :], memz_f[h][:, :]))(),
            reads=[r_memz_f[h]], writes=[r_memz_b[h]])
        P.op("vector", (lambda h=h: lambda e: e.tensor_mul(
            memb_beta[h][:, :], memz_f[h][:, :], betaB[h][:, :]))(),
            reads=[r_memz_f[h], r_betaB[h]], writes=[r_membeta[h]],
            tag="membeta0")

    xbf = [sba(f"xbf{i}", [128, NCH * SEG], BF16) for i in range(2)]
    r_xbf = [[R(f"xbf{i}c{c}") for c in range(NCH)] for i in range(2)]
    x2b = [sba(f"x2b{i}", [128, 512], BF16) for i in range(2)]
    r_x2b = [R(f"x2b{i}") for i in range(2)]
    rowf = sba("rowf", [1, 1024], F32)
    r_rowf = R("rowf", True)
    rows_b = [sba(f"rows_b{i}", [1, 1024], BF16) for i in range(2)]
    r_rows_b = [R(f"rows_b{i}", True) for i in range(2)]
    rbc = [sba(f"rbc{i}", [128, 512], BF16) for i in range(2)]
    r_rbc = [R(f"rbc{i}") for i in range(2)]

    def _bufs2(nm, w=512, dt=BF16):
        return ([[sba(f"{nm}{p}{h}", [128, w], dt) for h in range(HPC)]
                 for p in range(2)],
                [[R(f"{nm}{p}{h}") for h in range(HPC)] for p in range(2)])

    qS, r_qS = _bufs2("qS")
    kT, r_kT = _bufs2("kT")
    sq, r_sq = _bufs2("sq")
    skT, r_skT = _bufs2("skT")
    skt, r_skt = _bufs2("skt")
    vtk, r_vtk = _bufs2("vtk")
    vts, r_vts = _bufs2("vts")

    def _bufs1(nm, w=512, dt=BF16, strict=False):
        return ([sba(f"{nm}{h}", [128, w], dt) for h in range(HPC)],
                [R(f"{nm}{h}", strict) for h in range(HPC)])

    vdv, r_vdv = _bufs1("vdv")
    dlt, r_dlt = _bufs1("dlt", 4 * (DV + 1))
    sg1 = sba("sg1", [128, 512], BF16)
    sg2 = sba("sg2", [128, 512], BF16)
    r_sg1, r_sg2 = R("sg1"), R("sg2")
    Pb, r_Pb = _bufs1("Pb", 4 * 512)
    attq, r_attq = _bufs1("attq")
    attt, r_attt = _bufs1("attt")
    tmpc, r_tmpc = _bufs1("tmpc", 128, F32)
    zrot = sba("zrot", [128, 128], F32)
    r_zrot = R("zrot")
    P.op("vector", lambda e: e.memset(zrot[:, :], 0.0), writes=[r_zrot])
    rds, r_rds = _bufs1("rds", 8, F32, True)
    rnd, r_rnd = _bufs1("rnd", 4, F32, True)
    lnrow = sba("lnrow", [1, 512], F32)
    r_lnrow = R("lnrow", True)
    attP = [sba(f"attP{i}", [128, 4 * 512], BF16) for i in range(2)]
    r_attP = [R(f"attP{i}") for i in range(2)]
    outb = sba("outb", [128, NCH * 64], F32)
    r_outb = R("outb")

    for h in range(HPC):
        P.op("vector", (lambda h=h: lambda e: e.memset(
            dlt[h][:, :], -1.0))(), writes=[r_dlt[h]])

    def seg_dma(g):
        b = g % 2
        src = xT[:, g * SEG:(g + 1) * SEG].rearrange("(c p) s -> p c s", p=128)
        dst = xbf[b][:, :].rearrange("p (c s) -> p c s", s=SEG)
        P.op("gpsimd", (lambda src=src, dst=dst: lambda e: e.dma_start(
            out=dst, in_=src))(), reads=[r_xT], writes=r_xbf[b], dma=True,
            tag=f"xdma{g}")

    def stats_mm(g):
        b = g % 2
        for c in range(NCH):
            P.op("vector", (lambda b=b, c=c: lambda e: e.tensor_mul(
                x2b[c % 2][:, :], xbf[b][:, c * SEG:c * SEG + 512],
                xbf[b][:, c * SEG:c * SEG + 512]))(),
                reads=[r_xbf[b][c]], writes=[r_x2b[c % 2]], tag="x2")
            P.op("tensor", (lambda b=b, c=c: lambda e: e.matmul(
                p0[64:65, 0:512], ones_c[:, :], xbf[b][:, c * SEG:c * SEG + 512],
                start=(c == 0), stop=(c == NCH - 1)))(),
                reads=[r_xbf[b][c], r_ones_c], writes=[rp[0]], tag="mmM")
            P.op("tensor", (lambda c=c: lambda e: e.matmul(
                p1[64:65, 0:512], ones_c[:, :], x2b[c % 2][:, :],
                start=(c == 0), stop=(c == NCH - 1)))(),
                reads=[r_x2b[c % 2], r_ones_c], writes=[rp[1]], tag="mmS")

    def xscale(g):
        b = g % 2
        for c in range(NCH):
            P.op("vector", (lambda b=b, c=c: lambda e: e.tensor_mul(
                xbf[b][:, c * SEG:c * SEG + 512],
                xbf[b][:, c * SEG:c * SEG + 512], rbc[b][:, :]))(),
                reads=[r_xbf[b][c], r_rbc[b]], writes=[r_xbf[b][c]], tag="xr")

    def stats_evac(g):
        b = g % 2
        rf, rb = rowf, rows_b[b]
        rrf, rrb = r_rowf, r_rows_b[b]
        P.op("vector", (lambda rf=rf: lambda e: e.tensor_scalar_mul(
            rf[0:1, 0:512], p0[64:65, 0:512], 1.0 / D))(),
            reads=[rp[0]], writes=[rrf], tag="rowm")
        P.op("vector", (lambda rf=rf: lambda e: e.tensor_scalar_mul(
            rf[0:1, 512:1024], p1[64:65, 0:512], 1.0 / D))(),
            reads=[rp[1]], writes=[rrf], tag="rowsq")
        P.op("scalar", (lambda rf=rf: lambda e: e.activation(
            lnrow[0:1, :], rf[0:1, 0:512], AF.Square))(),
            reads=[rrf], writes=[r_lnrow], tag="msq")
        P.op("vector", (lambda rf=rf: lambda e: e.tensor_sub(
            rf[0:1, 512:1024], rf[0:1, 512:1024], lnrow[0:1, :]))(),
            reads=[rrf, r_lnrow], writes=[rrf], tag="var")
        P.op("scalar", (lambda rf=rf: lambda e: e.activation(
            rf[0:1, 512:1024], rf[0:1, 512:1024], AF.Ln,
            bias=epsc[0:1, 0:1]))(),
            reads=[rrf, r_epsc], writes=[rrf], tag="ln")
        P.op("scalar", (lambda rf=rf, rb=rb: lambda e: e.activation(
            rb[0:1, 0:512], rf[0:1, 512:1024], AF.Exp, scale=-0.5))(),
            reads=[rrf], writes=[rrb], tag="exp")
        P.op("scalar", (lambda rf=rf: lambda e: e.activation(
            lnrow[0:1, :], rf[0:1, 512:1024], AF.Exp, scale=-0.5))(),
            reads=[rrf], writes=[r_lnrow], tag="rexp")
        P.op("vector", (lambda rf=rf, rb=rb: lambda e: e.tensor_mul(
            rb[0:1, 512:1024], rf[0:1, 0:512], lnrow[0:1, :]))(),
            reads=[rrf, r_lnrow], writes=[rrb], tag="mrrow")

    def stats_rows(g):
        b = g % 2
        P.op("tensor", (lambda b=b: lambda e: e.matmul(
            p6[:, :], ones_r[:, :], rows_b[b][0:1, 0:512],
            start=True, stop=True))(),
            reads=[r_rows_b[b], r_ones_r], writes=[rp[6]], tag="rbc_mm")
        P.op("vector", (lambda b=b: lambda e: e.tensor_copy(
            rbc[b][:, :], p6[:, :]))(),
            reads=[rp[6]], writes=[r_rbc[b]], tag="rbc_evac")

    def proj(g, h, wbuf, r_wbuf, wslot, wsoff, dst, r_dst, pbank, rb_,
             scl=None):
        b = g % 2
        W = HPC * DK
        for c in range(NCH):
            P.op("tensor", (lambda c=c, wbuf=wbuf, wslot=wslot, b=b,
                            pbank=pbank: lambda e:
                 e.matmul(pbank[:, :], wbuf[:, c * W + wslot:c * W + wslot + 128],
                          xbf[b][:, c * SEG:c * SEG + 512],
                          start=(c == 0), stop=False))(),
                reads=[r_wbuf, r_xbf[b][c]], writes=[rb_], tag="prj")
        P.op("tensor", (lambda wsoff=wsoff, wslot=wslot, b=b,
                        pbank=pbank: lambda e: e.matmul(
            pbank[:, :], wsum[0:1, wsoff + wslot:wsoff + wslot + 128],
            rows_b[b][0:1, 512:1024], start=False, stop=True))(),
            reads=[r_wsum, r_rows_b[b]], writes=[rb_], tag="rank1")
        if scl is None:
            P.op("scalar", (lambda dst=dst, pbank=pbank: lambda e:
                 e.copy(dst[:, :], pbank[:, :]))(),
                reads=[rb_], writes=[r_dst], tag="pevac")
        else:
            P.op("scalar", (lambda dst=dst, scl=scl, pbank=pbank: lambda e:
                 e.mul(dst[:, :], pbank[:, :], scl))(),
                reads=[rb_], writes=[r_dst], tag="pevac")

    def sigma(src, r_src, dst, r_dst, comp=1.0):
        P.op("scalar", (lambda src=src: lambda e: e.activation(
            sg1[:, :], src[:, :], AF.Relu, scale=-1.0))(),
            reads=[r_src], writes=[r_sg1], tag="sgrelu1")
        P.op("scalar", (lambda comp=comp: lambda e: e.activation(
            sg2[:, :], sg1[:, :], AF.Exp, scale=-comp))(),
            reads=[r_sg1], writes=[r_sg2], tag="sgexp")
        P.op("scalar", (lambda src=src: lambda e: e.activation(
            sg1[:, :], src[:, :], AF.Relu))(),
            reads=[r_src], writes=[r_sg1], tag="sgrelu2")
        if comp == 1.0:
            P.op("vector", (lambda dst=dst: lambda e: e.tensor_add(
                dst[:, :], sg1[:, :], sg2[:, :]))(),
                reads=[r_sg1, r_sg2], writes=[r_dst], tag="sgadd")
        else:
            P.op("vector", (lambda dst=dst, comp=comp: lambda e:
                 e.scalar_tensor_tensor(dst[:, :], sg1[:, :], comp, sg2[:, :],
                                        ALU.mult, ALU.add))(),
                reads=[r_sg1, r_sg2], writes=[r_dst], tag="sgadd")

    def transp4(src, r_src, dst, r_dst):
        for t in range(4):
            P.op("tensor", (lambda t=t, src=src: lambda e: e.transpose(
                p7b[:, t * 128:(t + 1) * 128], src[:, t * 128:(t + 1) * 128],
                ident[:, :]))(),
                reads=[r_src, r_ident], writes=[r_p7s], tag="tr4")
        P.op("vector", (lambda dst=dst: lambda e: e.tensor_copy(
            dst[:, :], p7b[:, 0:512]))(),
            reads=[r_p7s], writes=[r_dst], tag="tr4_evac")

    def qkvA(g, h):
        b = g % 2
        ws = h * DK
        W = HPC * DK
        bk = [(p2, rp[2]), (p3, rp[3])]
        proj(g, h, wq_b, r_wq, ws, 0, qS[b][h], r_qS[b][h],
             *bk[(h * 3 + 0) % 2], scl=SCALE)
        proj(g, h, wk_b, r_wk, ws, W, kT[b][h], r_kT[b][h],
             *bk[(h * 3 + 1) % 2])
        proj(g, h, wv_b, r_wv, ws, 2 * W, vdv[h], r_vdv[h],
             *bk[(h * 3 + 2) % 2])

    def qkvSig(g, h):
        b = g % 2
        sigma(qS[b][h], r_qS[b][h], sq[b][h], r_sq[b][h], comp=math.sqrt(DK))
        sigma(kT[b][h], r_kT[b][h], skT[b][h], r_skT[b][h])

    def qkvTr(g, h):
        b = g % 2
        transp4(vdv[h], r_vdv[h], vtk[b][h], r_vtk[b][h])
        P.op("vector", (lambda b=b, h=h: lambda e: e.tensor_mul(
            vts[b][h][:, :], vtk[b][h][:, :], b1mB4[h][:, :]))(),
            reads=[r_vtk[b][h], r_b1mB4[h]], writes=[r_vts[b][h]], tag="vts")
        transp4(skT[b][h], r_skT[b][h], skt[b][h], r_skt[b][h])

    def sftT(g, h):
        b = g % 2
        for tc in range(4):
            n = 512 - tc * 128
            pbank, rb_ = (p0, rp[0]) if tc % 2 == 0 else (p1, rp[1])
            P.op("tensor", (lambda tc=tc, h=h, b=b, n=n, pbank=pbank: lambda e:
                 e.matmul(pbank[:, 0:n], kT[b][h][:, tc * 128:(tc + 1) * 128],
                          qS[b][h][:, tc * 128:512], start=True, stop=True))(),
                reads=[r_kT[b][h], r_qS[b][h]], writes=[rb_], tag="scoreT")
            P.op("scalar", (lambda tc=tc, h=h, n=n, pbank=pbank: lambda e:
                 e.activation(Pb[h][:, tc * 512 + tc * 128:tc * 512 + 512],
                              pbank[:, 0:n], AF.Exp, bias=negC[:, 0:1]))(),
                reads=[rb_, r_negC], writes=[r_Pb[h]], tag="expT")
            P.op("vector", (lambda tc=tc, h=h: lambda e: e.tensor_mul(
                Pb[h][:, tc * 512 + tc * 128:tc * 512 + tc * 128 + 128],
                Pb[h][:, tc * 512 + tc * 128:tc * 512 + tc * 128 + 128],
                trimask[:, :]))(),
                reads=[r_Pb[h], r_trimask], writes=[r_Pb[h]], tag="trimul")

    def memA(g, h):
        b = g % 2
        pM, rM = (p4, rp[4]) if h == 0 else (p5, rp[5])
        base = 384 + h * 16
        for sc in range(4):
            P.op("tensor", (lambda sc=sc, h=h, b=b, pM=pM: lambda e: e.matmul(
                pM[:, sc * 128:(sc + 1) * 128],
                sq[b][h][:, sc * 128:(sc + 1) * 128],
                memb_beta[h][:, 0:DK], start=True, stop=True))(),
                reads=[r_sq[b][h], r_membeta[h]], writes=[rM], tag="attM")
        for sc in range(4):
            P.op("tensor", (lambda sc=sc, h=h, b=b, base=base: lambda e:
                 e.matmul(p7[:, base + sc:base + sc + 1],
                          sq[b][h][:, sc * 128:(sc + 1) * 128],
                          memb_beta[h][:, DK:DK + 1], start=True, stop=True))(),
                reads=[r_sq[b][h], r_membeta[h]], writes=[r_p7t[h]], tag="attD")

    def dotB(g, h):
        b = g % 2
        base = 384 + h * 16
        for sc in range(4):
            for tc in range(sc + 1):
                P.op("tensor", (lambda sc=sc, tc=tc, h=h, b=b: lambda e:
                     e.matmul(p6[:, sc * 128:(sc + 1) * 128],
                              Pb[h][:, tc * 512 + sc * 128:
                                   tc * 512 + (sc + 1) * 128],
                              vts[b][h][:, tc * 128:(tc + 1) * 128],
                              start=(tc == 0), stop=(tc == sc)))(),
                    reads=[r_Pb[h], r_vts[b][h]], writes=[rp[6]], tag="attdot")
                P.op("tensor", (lambda sc=sc, tc=tc, h=h, base=base: lambda e:
                     e.matmul(p7[:, base + 4 + sc:base + 5 + sc],
                              Pb[h][:, tc * 512 + sc * 128:
                                   tc * 512 + (sc + 1) * 128],
                              ones_c[:, :], start=(tc == 0), stop=(tc == sc)))(),
                    reads=[r_Pb[h], r_ones_c], writes=[r_p7t[h]], tag="sume")
        P.op("vector", (lambda h=h, base=base: lambda e: e.reciprocal(
            rds[h][:, 0:8], p7[:, base:base + 8]))(),
            reads=[r_p7t[h]], writes=[r_rds[h]], tag="rcp_ds")

    def cmbT(g, h):
        pM, rM = (p4, rp[4]) if h == 0 else (p5, rp[5])
        for sc in range(4):
            P.op("vector", (lambda sc=sc, h=h: lambda e:
                 e.scalar_tensor_tensor(
                tmpc[h][:, :], p6[:, sc * 128:(sc + 1) * 128],
                rds[h][:, 4 + sc:5 + sc], zrot[:, :], ALU.mult, ALU.add))(),
                reads=[rp[6], r_rds[h], r_zrot], writes=[r_tmpc[h]], tag="cmbA")
            P.op("vector", (lambda sc=sc, h=h, pM=pM: lambda e:
                 e.scalar_tensor_tensor(
                attq[h][:, sc * 128:(sc + 1) * 128],
                pM[:, sc * 128:(sc + 1) * 128], rds[h][:, sc:sc + 1],
                tmpc[h][:, :], ALU.mult, ALU.add))(),
                reads=[rM, r_rds[h], r_tmpc[h]], writes=[r_attq[h]], tag="cmbB")
        for t in range(4):
            P.op("tensor", (lambda t=t, h=h: lambda e: e.transpose(
                p7b[:, t * 128:(t + 1) * 128], attq[h][:, t * 128:(t + 1) * 128],
                ident[:, :]))(),
                reads=[r_attq[h], r_ident], writes=[r_p7s], tag="attT")
        P.op("vector", (lambda h=h: lambda e: e.tensor_copy(
            attt[h][:, :], p7b[:, 0:512]))(),
            reads=[r_p7s], writes=[r_attt[h]], tag="attT_ev")
        pp = (g // 2) % 2
        blk = (g % 2) * 2 + h
        dstv = attP[pp][:, blk * 512:(blk + 1) * 512].rearrange(
            "p (c a) -> p a c", a=32)
        srcv = attt[h][:, :].rearrange("p (a c) -> p a c", c=16)
        P.op("scalar", (lambda dstv=dstv, srcv=srcv: lambda e: e.copy(
            dstv, srcv))(),
            reads=[r_attt[h]], writes=[r_attP[pp]], tag="attT_pk")

    def deltaC(g, h):
        b = g % 2
        pM, rM = (p4, rp[4]) if h == 0 else (p5, rp[5])
        base = 384 + h * 16
        for sc in range(4):
            P.op("tensor", (lambda sc=sc, h=h, b=b, pM=pM: lambda e: e.matmul(
                pM[:, sc * 128:(sc + 1) * 128],
                skT[b][h][:, sc * 128:(sc + 1) * 128], memz_b[h][:, 0:DK],
                start=True, stop=True))(),
                reads=[r_skT[b][h], r_memz_b[h]], writes=[rM], tag="nk")
        for sc in range(4):
            P.op("tensor", (lambda sc=sc, h=h, b=b, base=base: lambda e:
                 e.matmul(p7[:, base + 8 + sc:base + 9 + sc],
                          skT[b][h][:, sc * 128:(sc + 1) * 128],
                          memz_b[h][:, DK:DK + 1], start=True, stop=True))(),
                reads=[r_skT[b][h], r_memz_b[h]], writes=[r_p7t[h]], tag="nd")
        P.op("vector", (lambda h=h, base=base: lambda e: e.reciprocal(
            rnd[h][:, :], p7[:, base + 8:base + 12]))(),
            reads=[r_p7t[h]], writes=[r_rnd[h]], tag="rcp_nd")
        for sc in range(4):
            P.op("vector", (lambda sc=sc, h=h, b=b, pM=pM: lambda e:
                 e.scalar_tensor_tensor(
                dlt[h][:, sc * 129:sc * 129 + 128],
                pM[:, sc * 128:(sc + 1) * 128], rnd[h][:, sc:sc + 1],
                vtk[b][h][:, sc * 128:(sc + 1) * 128],
                ALU.mult, ALU.subtract))(),
                reads=[rM, r_rnd[h], r_vtk[b][h]], writes=[r_dlt[h]],
                tag="delta")
        for sc in range(4):
            P.op("tensor", (lambda sc=sc, h=h, b=b, pM=pM: lambda e: e.matmul(
                pM[:, 0:129], skt[b][h][:, sc * 128:(sc + 1) * 128],
                dlt[h][:, sc * 129:sc * 129 + 129],
                start=(sc == 0), stop=(sc == 3)))(),
                reads=[r_skt[b][h], r_dlt[h]], writes=[rM], tag="upd")
        P.op("vector", (lambda h=h, pM=pM: lambda e: e.tensor_sub(
            memz_f[h][:, :], memz_f[h][:, :], pM[:, 0:129]))(),
            reads=[rM, r_memz_f[h]], writes=[r_memz_f[h]], tag="memupd")
        P.op("scalar", (lambda h=h: lambda e: e.copy(
            memz_b[h][:, :], memz_f[h][:, :]))(),
            reads=[r_memz_f[h]], writes=[r_memz_b[h]], tag="memcast")
        P.op("vector", (lambda h=h: lambda e: e.tensor_mul(
            memb_beta[h][:, :], memz_f[h][:, :], betaB[h][:, :]))(),
            reads=[r_memz_f[h], r_betaB[h]], writes=[r_membeta[h]],
            tag="membeta")

    def wo_pair(p_idx):
        pp = p_idx % 2
        attv = attP[pp][:, :].rearrange("p (b s) -> p b s", s=512)
        for half in range(2):
            for eg in range(2):
                pbank, rb_ = (p0, rp[0]) if eg % 2 == 0 else (p1, rp[1])
                for ei in range(4):
                    e_ch = half * 8 + eg * 4 + ei
                    for j in range(NCH):
                        P.op("tensor", (lambda e_ch=e_ch, ei=ei, j=j,
                                        pbank=pbank, attv=attv: lambda e:
                             e.matmul(
                            pbank[:, ei * 128:(ei + 1) * 128],
                            wo_b[:, j * D + e_ch * 128:j * D + (e_ch + 1) * 128],
                            attv[:, :, j * 32:(j + 1) * 32],
                            start=(j == 0), stop=(j == NCH - 1)))(),
                            reads=[r_wo, r_attP[pp]], writes=[rb_], tag="womm")
                P.op("vector", (lambda eg=eg, pbank=pbank: lambda e:
                     e.tensor_copy(
                    outb[:, eg * 512:(eg + 1) * 512], pbank[:, :]))(),
                    reads=[rb_], writes=[r_outb], tag="woev")
            dst = att_scr[half * 1024:(half + 1) * 1024,
                          p_idx * 128:(p_idx + 1) * 128].rearrange(
                "(c p) r -> p c r", p=128)
            srcv = outb[:, :].rearrange("p (c r) -> p c r", r=128)
            P.op("sync", (lambda dst=dst, srcv=srcv: lambda e: e.dma_start(
                out=dst, in_=srcv))(), reads=[r_outb], writes=[r_scr],
                dma=True, tag="scrdma")

    if nseg > 0:
        seg_dma(0)
        stats_mm(0)
        stats_evac(0)
        stats_rows(0)
        xscale(0)
        qkvA(0, 0)
        qkvA(0, 1)
        seg_dma(1)
        wo_load()
        qkvSig(0, 0)
        qkvSig(0, 1)
        sftT(0, 0)
        sftT(0, 1)
        qkvTr(0, 0)
        qkvTr(0, 1)
    for g in range(nseg):
        nxt = g + 1 < nseg
        if nxt:
            stats_mm(g + 1)
        memA(g, 0)
        dotB(g, 0)
        if nxt:
            stats_evac(g + 1)
        cmbT(g, 0)
        memA(g, 1)
        dotB(g, 1)
        cmbT(g, 1)
        deltaC(g, 0)
        if nxt:
            stats_rows(g + 1)
            xscale(g + 1)
            qkvA(g + 1, 0)
            qkvSig(g + 1, 0)
        deltaC(g, 1)
        if nxt:
            qkvA(g + 1, 1)
            qkvSig(g + 1, 1)
            sftT(g + 1, 0)
            sftT(g + 1, 1)
            qkvTr(g + 1, 0)
            qkvTr(g + 1, 1)
        if g % 2 == 1:
            wo_pair(g // 2)
        if g + 2 < nseg:
            seg_dma(g + 2)

    P.fence()
    esa.close()
    xres = sb("xres", [128, NCH * ROWS], F32)
    xln = sb("xln", [128, NCH * ROWS], BF16)
    hbf = sb("hbf", [128, 8 * ROWS], BF16)
    QSUP = SUPI // 4
    wgg = [sb(f"wgg{i}", [128, NCH * QSUP], BF16) for i in range(2)]
    wgu = [sb(f"wgu{i}", [128, NCH * QSUP], BF16) for i in range(2)]
    wdb = sb("wdb", [128, 8 * D], BF16)
    sgf = sb("sgf", [128, 2048], F32)
    x2f = sb("x2f", [128, 1024], BF16)
    rowF = sb("rowF", [1, 2048], F32)
    rowFb = sb("rowFb", [1, 2048], BF16)
    rbc2 = sb("rbc2", [128, 1024], F32)
    mbc2 = sb("mbc2", [128, 1024], F32)
    r_xres, r_xln, r_hbf = R("xres"), R("xln"), R("hbf")
    r_wgg = [R(f"wgg{i}") for i in range(2)]
    r_wgu = [R(f"wgu{i}") for i in range(2)]
    r_wdb = R("wdb")
    r_sgf, r_x2f = R("sgf"), R("x2f")
    r_rowF, r_rowFb = R("rowF", True), R("rowFb", True)
    r_rbc2, r_mbc2 = R("rbc2"), R("mbc2")

    r_xresg = [R(f"xresg{i}") for i in range(4)]
    for grp in range(4):
        cs = slice(grp * 4 * ROWS, (grp + 1) * 4 * ROWS)
        P.op("sync", (lambda grp=grp, cs=cs: lambda e: e.dma_start(
            out=xres[:, cs].rearrange("p (c s) -> p c s", s=ROWS),
            in_=xrt[grp * 512:(grp + 1) * 512, :].rearrange(
                "(c p) s -> p c s", p=128)))(),
            writes=[r_xresg[grp]], dma=True, tag="xrt_dma")
        P.op("gpsimd", (lambda grp=grp, cs=cs: lambda e: e.dma_start(
            out=xres[:, cs].rearrange("p (c s) -> p c s", s=ROWS),
            in_=att_scr[grp * 512:(grp + 1) * 512, :].rearrange(
                "(c p) s -> p c s", p=128), accum_op=ALU.add))(),
            reads=[r_scr, r_xresg[grp]], writes=[r_xresg[grp]], dma=True,
            tag="scr_acc")
        for c in range(grp * 4, grp * 4 + 4):
            P.op("scalar", (lambda c=c, grp=grp: lambda e: e.copy(
                xln[:, c * ROWS:(c + 1) * ROWS],
                xres[:, c * ROWS:(c + 1) * ROWS]))(),
                reads=[r_xresg[grp]], writes=[r_xln], tag="xcast")
    P.op("vector", lambda e: e.nop(),
         reads=[r_xresg[0], r_xresg[1], r_xresg[2], r_xresg[3]],
         writes=[r_xres], tag="xres_join")
    for c in range(NCH):
        P.op("scalar", (lambda c=c: lambda e: e.activation(
            x2f[:, :], xln[:, c * ROWS:(c + 1) * ROWS], AF.Square))(),
            reads=[r_xln], writes=[r_x2f], tag="fx2")
        for sh in range(2):
            P.op("tensor", (lambda c=c, sh=sh: lambda e: e.matmul(
                p0[0:1, 0:512] if sh == 0 else p2[0:1, 0:512], ones_c[:, :],
                xln[:, c * ROWS + sh * 512:c * ROWS + sh * 512 + 512],
                start=(c == 0), stop=(c == NCH - 1)))(),
                reads=[r_xln, r_ones_c], writes=[rp[0] if sh == 0 else rp[2]],
                tag="fmmM")
            P.op("tensor", (lambda c=c, sh=sh: lambda e: e.matmul(
                p1[0:1, 0:512] if sh == 0 else p3[0:1, 0:512], ones_c[:, :],
                x2f[:, sh * 512:sh * 512 + 512],
                start=(c == 0), stop=(c == NCH - 1)))(),
                reads=[r_x2f, r_ones_c], writes=[rp[1] if sh == 0 else rp[3]],
                tag="fmmS")
    for sh, (pm, pv, rm, rv) in enumerate(((p0, p1, rp[0], rp[1]),
                                           (p2, p3, rp[2], rp[3]))):
        o = sh * 512
        P.op("vector", (lambda o=o, pm=pm: lambda e: e.tensor_scalar_mul(
            rowF[0:1, o:o + 512], pm[0:1, 0:512], 1.0 / D))(),
            reads=[rm], writes=[r_rowF], tag="frowm")
        P.op("vector", (lambda o=o: lambda e: e.tensor_mul(
            rowF[0:1, 1024 + o:1024 + o + 512], rowF[0:1, o:o + 512],
            rowF[0:1, o:o + 512]))(),
            reads=[r_rowF], writes=[r_rowF], tag="fmsq")
        P.op("vector", (lambda o=o, pv=pv: lambda e: e.scalar_tensor_tensor(
            rowF[0:1, 1024 + o:1024 + o + 512], pv[0:1, 0:512], 1.0 / D,
            rowF[0:1, 1024 + o:1024 + o + 512], ALU.mult, ALU.subtract))(),
            reads=[rv, r_rowF], writes=[r_rowF], tag="fvar")
    P.op("scalar", lambda e: e.activation(rowF[0:1, 1024:2048],
                                          rowF[0:1, 1024:2048], AF.Ln,
                                          bias=epsc[0:1, 0:1]),
         reads=[r_rowF, r_epsc], writes=[r_rowF], tag="fln")
    P.op("scalar", lambda e: e.activation(rowF[0:1, 1024:2048],
                                          rowF[0:1, 1024:2048], AF.Exp,
                                          scale=-0.5),
         reads=[r_rowF], writes=[r_rowF], tag="fexp")
    P.op("vector", lambda e: e.tensor_mul(rowF[0:1, 0:1024], rowF[0:1, 0:1024],
                                          rowF[0:1, 1024:2048]),
         reads=[r_rowF], writes=[r_rowF], tag="fmr")
    P.op("scalar", lambda e: e.copy(rowFb[0:1, :], rowF[0:1, :]),
         reads=[r_rowF], writes=[r_rowFb], tag="frowb")
    for sh in range(2):
        P.op("tensor", (lambda sh=sh: lambda e: e.matmul(
            p5[:, 0:512], ones_r[:, :],
            rowFb[0:1, 1024 + sh * 512:1024 + sh * 512 + 512],
            start=True, stop=True))(),
            reads=[r_rowFb, r_ones_r], writes=[rp[5]], tag="frbc")
        P.op("vector", (lambda sh=sh: lambda e: e.tensor_copy(
            rbc2[:, sh * 512:sh * 512 + 512], p5[:, 0:512]))(),
            reads=[rp[5]], writes=[r_rbc2], tag="frbc_ev")
        P.op("tensor", (lambda sh=sh: lambda e: e.matmul(
            p6[:, 0:512], ones_r[:, :],
            rowFb[0:1, sh * 512:sh * 512 + 512], start=True, stop=True))(),
            reads=[r_rowFb, r_ones_r], writes=[rp[6]], tag="fmbc")
        P.op("vector", (lambda sh=sh: lambda e: e.tensor_copy(
            mbc2[:, sh * 512:sh * 512 + 512], p6[:, 0:512]))(),
            reads=[rp[6]], writes=[r_mbc2], tag="fmbc_ev")
    for c in range(NCH):
        P.op("vector", (lambda c=c: lambda e: e.tensor_mul(
            sgf[:, 0:1024], xres[:, c * ROWS:(c + 1) * ROWS], rbc2[:, :]))(),
            reads=[r_xres, r_rbc2], writes=[r_sgf], tag="fnorm1")
        P.op("vector", (lambda c=c: lambda e: e.tensor_sub(
            xln[:, c * ROWS:(c + 1) * ROWS], sgf[:, 0:1024], mbc2[:, :]))(),
            reads=[r_sgf, r_mbc2], writes=[r_xln], tag="fnorm2")

    def gate_dma(sup, quarter):
        sl = quarter % 2
        o0 = sup * SUPI + quarter * QSUP
        gsrc = wgut[:, o0:o0 + QSUP].rearrange("(c p) o -> p c o", p=128)
        usrc = wgut[:, 8192 + o0:8192 + o0 + QSUP].rearrange(
            "(c p) o -> p c o", p=128)
        P.op("gpsimd", (lambda gsrc=gsrc, sl=sl: lambda e: e.dma_start(
            out=wgg[sl][:, :].rearrange("p (c o) -> p c o", o=QSUP),
            in_=gsrc))(), writes=[r_wgg[sl]], dma=True, tag="wgg_dma")
        P.op("gpsimd", (lambda usrc=usrc, sl=sl: lambda e: e.dma_start(
            out=wgu[sl][:, :].rearrange("p (c o) -> p c o", o=QSUP),
            in_=usrc))(), writes=[r_wgu[sl]], dma=True, tag="wgu_dma")

    def wd_dma(sup):
        wsrc = wdt[sup * SUPI:(sup + 1) * SUPI, :].rearrange(
            "(b p) h2 -> p b h2", p=128)
        P.op("gpsimd", (lambda wsrc=wsrc: lambda e: e.dma_start(
            out=wdb[:, :].rearrange("p (b h2) -> p b h2", h2=D), in_=wsrc))(),
            writes=[r_wdb], dma=True, tag="wd_dma")

    gate_dma(0, 0)
    gate_dma(0, 1)
    wd_dma(0)
    for sup in range(nsup):
        for quarter in range(4):
            sl = quarter % 2
            for jj in range(2):
                j = quarter * 2 + jj
                for sh in range(2):
                    k4 = (j % 2) * 4
                    pg, rg = psums[k4 + 2 * sh], rp[k4 + 2 * sh]
                    pu, ru = psums[k4 + 2 * sh + 1], rp[k4 + 2 * sh + 1]
                    for c in range(NCH):
                        P.op("tensor", (lambda c=c, jj=jj, sh=sh, pg=pg, sl=sl:
                             lambda e: e.matmul(
                            pg[:, :],
                            wgg[sl][:, c * QSUP + jj * 128:c * QSUP + (jj + 1) * 128],
                            xln[:, c * ROWS + sh * 512:c * ROWS + sh * 512 + 512],
                            start=(c == 0), stop=(c == NCH - 1)))(),
                            reads=[r_wgg[sl], r_xln], writes=[rg], tag="gmm")
                        P.op("tensor", (lambda c=c, jj=jj, sh=sh, pu=pu, sl=sl:
                             lambda e: e.matmul(
                            pu[:, :],
                            wgu[sl][:, c * QSUP + jj * 128:c * QSUP + (jj + 1) * 128],
                            xln[:, c * ROWS + sh * 512:c * ROWS + sh * 512 + 512],
                            start=(c == 0), stop=(c == NCH - 1)))(),
                            reads=[r_wgu[sl], r_xln], writes=[ru], tag="umm")
                    P.op("scalar", (lambda sh=sh, pg=pg, j=j: lambda e:
                         e.activation(
                        sgf[:, (j % 2) * 1024 + sh * 512:
                            (j % 2) * 1024 + sh * 512 + 512],
                        pg[:, :], AF.Silu))(),
                        reads=[rg], writes=[r_sgf], tag="silu")
                    P.op("vector", (lambda j=j, sh=sh, pu=pu: lambda e:
                         e.tensor_mul(
                        hbf[:, j * ROWS + sh * 512:j * ROWS + sh * 512 + 512],
                        sgf[:, (j % 2) * 1024 + sh * 512:
                            (j % 2) * 1024 + sh * 512 + 512], pu[:, :]))(),
                        reads=[r_sgf, ru], writes=[r_hbf], tag="hmul")
            qn = sup * 4 + quarter + 2
            if qn < nsup * 4:
                gate_dma(qn // 4, qn % 4)

        for eg in range(8):
            for ei in range(2):
                e_ch = eg * 2 + ei
                for sh in range(2):
                    pd, rd_ = ((p4, rp[4]), (p5, rp[5]),
                               (p6, rp[6]), (p7, rp[7]))[ei * 2 + sh]
                    for b in range(8):
                        P.op("tensor", (lambda e_ch=e_ch, sh=sh, b=b, pd=pd:
                             lambda e: e.matmul(
                            pd[:, :],
                            wdb[:, b * D + e_ch * 128:b * D + (e_ch + 1) * 128],
                            hbf[:, b * ROWS + sh * 512:b * ROWS + sh * 512 + 512],
                            start=(b == 0), stop=(b == 7)))(),
                            reads=[r_wdb, r_hbf], writes=[rd_], tag="dmm")
                    P.op("vector", (lambda e_ch=e_ch, sh=sh, pd=pd: lambda e:
                         e.tensor_add(
                        xres[:, e_ch * ROWS + sh * 512:e_ch * ROWS + sh * 512 + 512],
                        xres[:, e_ch * ROWS + sh * 512:e_ch * ROWS + sh * 512 + 512],
                        pd[:, :]))(),
                        reads=[rd_, r_xres], writes=[r_xres], tag="dacc")
        if sup + 1 < nsup:
            wd_dma(sup + 1)

    P.op("sync", lambda e: e.dma_start(
        out=outT.rearrange("(c p) s -> p c s", p=128),
        in_=xres[:, :].rearrange("p (c s) -> p c s", s=ROWS)),
        reads=[r_xres], writes=[r_outT], dma=True, tag="out_dma")
    P.op("gpsimd", lambda e: e.nop(), reads=[r_outT, r_scr], tag="final")

    P.emit(nc)
    es.close()
    return nc


def _shard_inputs(L0_attn_x, L0_attn_Wq, L0_attn_Wk, L0_attn_Wv, L0_attn_Wo,
                  L0_attn_betas, L0_ffn_w_gate_up, L0_ffn_w_down):
    x = np.ascontiguousarray(np.asarray(L0_attn_x, np.float32)[0])
    xT = np.ascontiguousarray(x.T)
    Wq = np.asarray(L0_attn_Wq, np.float32)
    Wk = np.asarray(L0_attn_Wk, np.float32)
    Wv = np.asarray(L0_attn_Wv, np.float32)
    Wo = np.ascontiguousarray(np.asarray(L0_attn_Wo, np.float32))
    betas = np.asarray(L0_attn_betas, np.float32)[0, :, 0, :]
    wgut = np.ascontiguousarray(np.asarray(L0_ffn_w_gate_up, np.float32).T)
    wdt = np.ascontiguousarray(np.asarray(L0_ffn_w_down, np.float32).T)

    in_maps = []
    row_idx = []
    for c in range(NC):
        rows = (np.arange(NSEG)[:, None] * SEG + 64 * c
                + np.arange(64)[None, :]).reshape(-1)
        row_idx.append(rows)
        in_maps.append({
            "xT": xT,
            "xrt": np.ascontiguousarray(x[rows].T),
            "wq": np.ascontiguousarray(Wq[:, c * HPC * DK:(c + 1) * HPC * DK]),
            "wk": np.ascontiguousarray(Wk[:, c * HPC * DK:(c + 1) * HPC * DK]),
            "wv": np.ascontiguousarray(Wv[:, c * HPC * DV:(c + 1) * HPC * DV]),
            "wo": Wo,
            "wgut": wgut,
            "wdt": wdt,
            "betasT": np.ascontiguousarray(betas[c * HPC:(c + 1) * HPC].T),
        })
    return in_maps, row_idx


def kernel(**inputs):
    in_maps, row_idx = _shard_inputs(**inputs)
    nc = build_program()
    res = run_bass_kernel_spmd(nc, in_maps, core_ids=list(range(NC)))
    out = np.empty((1, S, D), np.float32)
    for c in range(NC):
        out[0, row_idx[c]] = res.results[c]["outT"].T
    return out


# version 13
# speedup vs baseline: 1.0565x; 1.0565x over previous
import contextlib
import math

import numpy as np

import concourse.bass as bass
import concourse.mybir as mybir
from concourse.bass_utils import run_bass_kernel_spmd

F32 = mybir.dt.float32
BF16 = mybir.dt.bfloat16
AF = mybir.ActivationFunctionType
ALU = mybir.AluOpType
AX = mybir.AxisListType

S, D, H, DK, DV, SEG, NSEG = 8192, 2048, 16, 128, 128, 512, 16
NC = 8
HPC = H // NC
ROWS = S // NC
NCH = D // 128
SCALE = 1.0 / math.sqrt(DK)
EPS = 1e-5
NSUP = 8
SUPI = 8192 // NSUP

EXPB = -20.0


class _Res:
    __slots__ = ("w", "r", "strict", "name")

    def __init__(self, name="", strict=False):
        self.w = {}
        self.r = {}
        self.strict = strict
        self.name = name


class _Op:
    __slots__ = ("eng", "dom", "dma", "builder", "ord", "deps", "clock",
                 "needs_inc", "val", "waits", "tag")


class _Plan:
    ENGS = ("sync", "gpsimd", "vector", "scalar", "tensor")

    def __init__(self):
        self.eng_ops = {e: [] for e in self.ENGS}
        self.dom_ord = {}
        self.dom_last = {}
        self.all = []

    def op(self, eng, builder, reads=(), writes=(), dma=False, tag="",
           after=()):
        o = _Op()
        o.eng, o.dma, o.builder, o.tag = eng, dma, builder, tag
        o.dom = ("D" + eng) if dma else eng
        o.deps = {}
        o.needs_inc = False
        o.ord = self.dom_ord.get(o.dom, 0) + 1
        self.dom_ord[o.dom] = o.ord

        def dep(p, force):
            if p is None or p is o:
                return
            if (not force) and p.dom == o.dom and (not p.dma) and (not o.dma):
                return
            cur = o.deps.get(p.dom)
            if cur is None or p.ord > cur.ord:
                o.deps[p.dom] = p

        for p in after:
            dep(p, True)
        for res in reads:
            for p in res.w.values():
                dep(p, res.strict)
        for res in writes:
            for p in res.w.values():
                dep(p, res.strict)
            for p in res.r.values():
                dep(p, res.strict or p.dma or o.dma)
        for res in reads:
            res.r[o.dom] = o
        for res in writes:
            res.w = {o.dom: o}
            res.r = {}
        self.eng_ops[eng].append(o)
        self.all.append(o)
        self.dom_last[o.dom] = o
        return o

    def fence(self):
        tails = list(self.dom_last.values())
        for eng in self.ENGS:
            self.op(eng, lambda e: e.nop(), after=tails, tag="fence")

    def resolve(self):
        last_clock = {}
        for o in self.all:
            ck = dict(last_clock.get(o.eng, ()))
            o.waits = []
            for dom in sorted(o.deps):
                p = o.deps[dom]
                if ck.get(dom, 0) >= p.ord:
                    continue
                o.waits.append(p)
                p.needs_inc = True
                for d2, v2 in p.clock.items():
                    if ck.get(d2, 0) < v2:
                        ck[d2] = v2
                if ck.get(dom, 0) < p.ord:
                    ck[dom] = p.ord
            o.clock = ck
            last_clock[o.eng] = ck
        cnt = {}
        for o in self.all:
            if o.dma:
                o.needs_inc = True
            if o.needs_inc:
                v = cnt.get(o.dom, 0) + (16 if o.dma else 1)
                cnt[o.dom] = v
                o.val = v

    def emit(self, nc):
        self.resolve()
        doms = sorted({o.dom for o in self.all if o.needs_inc}
                      | {o.dom for o in self.all for p in [None]}
                      | {p.dom for o in self.all for p in o.waits})
        with contextlib.ExitStack() as es:
            sems = {d: es.enter_context(nc.semaphore(f"sem_{d}")) for d in doms}
            block = es.enter_context(nc.Block())

            def section(engname):
                ops = self.eng_ops[engname]

                def run(e):
                    for o in ops:
                        for p in o.waits:
                            e.wait_ge(sems[p.dom], p.val)
                        ins = o.builder(e)
                        if o.tag:
                            ins.annotate(o.tag)
                        if o.needs_inc:
                            ins.then_inc(sems[o.dom], 16 if o.dma else 1)
                return run

            for engname in self.ENGS:
                if self.eng_ops[engname]:
                    getattr(block, engname)(section(engname))


def build_program(nseg=NSEG, nsup=NSUP, scr_out=False):
    nc = bass.Bass("TRN2", target_bir_lowering=False)

    xT = nc.dram_tensor("xT", [D, S], F32, kind="ExternalInput").ap()
    xrt = nc.dram_tensor("xrt", [D, ROWS], F32, kind="ExternalInput").ap()
    wq = nc.dram_tensor("wq", [D, HPC * DK], F32, kind="ExternalInput").ap()
    wk = nc.dram_tensor("wk", [D, HPC * DK], F32, kind="ExternalInput").ap()
    wv = nc.dram_tensor("wv", [D, HPC * DV], F32, kind="ExternalInput").ap()
    wo = nc.dram_tensor("wo", [D, D], F32, kind="ExternalInput").ap()
    wgut = nc.dram_tensor("wgut", [D, 2 * 8192], F32, kind="ExternalInput").ap()
    wdt = nc.dram_tensor("wdt", [8192, D], F32, kind="ExternalInput").ap()
    betasT = nc.dram_tensor("betasT", [128, HPC], F32, kind="ExternalInput").ap()
    outT = nc.dram_tensor("outT", [D, ROWS], F32, kind="ExternalOutput").ap()

    scr_kind = "ExternalOutput" if scr_out else "Internal"
    att_scr = nc.dram_tensor("att_scr", [D, ROWS], F32, kind=scr_kind).ap()

    import ml_dtypes
    tri_np = np.triu(np.ones((128, 128), np.float32)).astype(ml_dtypes.bfloat16)
    tri_dram = nc.inline_tensor(tri_np, "tri_c").ap()

    P = _Plan()
    R = lambda name, strict=False: _Res(name, strict)

    es = contextlib.ExitStack()
    esa = contextlib.ExitStack()
    sb = lambda name, shape, dt: es.enter_context(nc.sbuf_tensor(name, shape, dt))
    sba = lambda name, shape, dt: esa.enter_context(nc.sbuf_tensor(name, shape, dt))
    ps = lambda name, shape, dt=F32: es.enter_context(nc.psum_tensor(name, shape, dt))

    ident = sb("ident", [128, 128], BF16)
    ones_c = sb("ones_c", [128, 1], BF16)
    ones_r = sb("ones_r", [1, 128], BF16)
    epsc = sb("epsc", [1, 1], F32)
    identf = sba("identf", [128, 128], F32)
    ones_rf = sba("ones_rf", [1, 128], F32)
    trimask = sba("trimask", [128, 128], BF16)
    negC = sba("negC", [128, 1], F32)
    beta = sba("beta", [128, HPC], F32)
    browf = sba("browf", [1, 2 * 128], F32)
    b1rowf = sba("b1rowf", [1, 2 * 128], F32)
    wq_b = sba("wq_b", [128, NCH * HPC * DK], BF16)
    wk_b = sba("wk_b", [128, NCH * HPC * DK], BF16)
    wv_b = sba("wv_b", [128, NCH * HPC * DV], BF16)
    wo_b = sba("wo_b", [128, NCH * D], BF16)
    wsum = sba("wsum", [1, 3 * HPC * DK], BF16)
    memz_f = [sba(f"memz_f{h}", [128, DK + 1], F32) for h in range(HPC)]
    memz_b = [sba(f"memz_b{h}", [128, DK + 1], BF16) for h in range(HPC)]
    betaB = [sba(f"betaB{h}", [128, DK + 1], F32) for h in range(HPC)]
    b1mB4 = [sba(f"b1mB4{h}", [128, 512], BF16) for h in range(HPC)]
    memb_beta = [sba(f"membeta{h}", [128, DK + 1], BF16) for h in range(HPC)]

    r_ident = R("ident")
    r_identf = R("identf")
    r_epsc = R("epsc", True)
    r_negC = R("negC", True)
    r_ones_c, r_ones_r, r_ones_rf = R("ones_c"), R("ones_r"), R("ones_rf")
    r_trimask = R("trimask")
    r_beta = R("beta", True)
    r_browf, r_b1rowf = R("browf", True), R("b1rowf", True)
    r_wq, r_wk, r_wv, r_wo = R("wq_b"), R("wk_b"), R("wv_b"), R("wo_b")
    r_wsum = R("wsum", True)
    r_memz_f = [R(f"memz_f{h}") for h in range(HPC)]
    r_memz_b = [R(f"memz_b{h}") for h in range(HPC)]
    r_betaB = [R(f"betaB{h}") for h in range(HPC)]
    r_b1mB4 = [R(f"b1mB4{h}") for h in range(HPC)]
    r_membeta = [R(f"membeta{h}") for h in range(HPC)]
    r_xT = R("xT")
    r_scr = R("att_scr")
    r_outT = R("outT")

    p0 = ps("p0", [128, 512])
    p1 = ps("p1", [128, 512])
    p2 = ps("p2", [128, 512])
    p3 = ps("p3", [128, 512])
    p4 = ps("p4", [128, 512])
    p5 = ps("p5", [128, 512])
    p6 = ps("p6", [128, 512])
    p7 = ps("p7", [128, 512])
    p7b = p7[:, :].bitcast(BF16)
    psums = [p0, p1, p2, p3, p4, p5, p6, p7]
    rp = [R(f"p{i}") for i in range(8)]
    r_p7s = rp[7]
    r_p7t = [rp[7] for _ in range(HPC)]

    P.op("gpsimd", lambda e: e.memset(ident[:, :], 0.0), writes=[r_ident])
    P.op("gpsimd", lambda e: e.affine_select(
        out=ident[:, :], in_=ident[:, :], compare_op=ALU.not_equal, fill=1.0,
        base=0, pattern=[[-1, 128]], channel_multiplier=1),
        reads=[r_ident], writes=[r_ident])
    P.op("gpsimd", lambda e: e.memset(identf[:, :], 0.0), writes=[r_identf])
    P.op("gpsimd", lambda e: e.affine_select(
        out=identf[:, :], in_=identf[:, :], compare_op=ALU.not_equal, fill=1.0,
        base=0, pattern=[[-1, 128]], channel_multiplier=1),
        reads=[r_identf], writes=[r_identf])
    P.op("gpsimd", lambda e: e.memset(ones_c[:, :], 1.0), writes=[r_ones_c])
    P.op("gpsimd", lambda e: e.memset(ones_r[:, :], 1.0), writes=[r_ones_r])
    P.op("gpsimd", lambda e: e.memset(ones_rf[:, :], 1.0), writes=[r_ones_rf])
    P.op("gpsimd", lambda e: e.memset(epsc[:, :], EPS), writes=[r_epsc])
    P.op("gpsimd", lambda e: e.memset(negC[:, :], EXPB), writes=[r_negC])
    P.op("sync", lambda e: e.dma_start(out=trimask[:, :], in_=tri_dram[:, :]),
         writes=[r_trimask], dma=True)

    P.op("sync", lambda e: e.dma_start(out=beta[:, :], in_=betasT[:, :]),
         writes=[r_beta], dma=True)
    P.op("scalar", lambda e: e.activation(beta[:, :], beta[:, :], AF.Sigmoid),
         reads=[r_beta], writes=[r_beta])
    for h in range(HPC):
        P.op("tensor", (lambda h=h: lambda e: e.matmul(
            p7[0:1, 256:384], beta[:, h:h + 1], identf[:, :],
            start=True, stop=True))(),
            reads=[r_beta, r_identf], writes=[r_p7s], tag="brow_mm")
        P.op("vector", (lambda h=h: lambda e: e.tensor_copy(
            browf[0:1, h * 128:(h + 1) * 128], p7[0:1, 256:384]))(),
            reads=[r_p7s], writes=[r_browf], tag="brow_ev")
    P.op("vector", lambda e: e.tensor_scalar(b1rowf[0:1, :], browf[0:1, :],
                                             -1.0, 1.0, ALU.mult, ALU.add),
         reads=[r_browf], writes=[r_b1rowf])
    for h in range(HPC):
        P.op("tensor", (lambda h=h: lambda e: e.matmul(
            p2[:, 0:128], ones_rf[:, :], browf[0:1, h * 128:(h + 1) * 128],
            start=True, stop=True))(),
            reads=[r_ones_rf, r_browf], writes=[rp[2]], tag="betabc")
        P.op("vector", (lambda h=h: lambda e: e.tensor_copy(
            betaB[h][:, 0:128], p2[:, 0:128]))(),
            reads=[rp[2]], writes=[r_betaB[h]], tag="betabc_ev")
        P.op("vector", (lambda h=h: lambda e: e.memset(
            betaB[h][:, 128:129], 1.0))(),
            reads=[r_betaB[h]], writes=[r_betaB[h]])
        P.op("tensor", (lambda h=h: lambda e: e.matmul(
            p3[:, 0:128], ones_rf[:, :], b1rowf[0:1, h * 128:(h + 1) * 128],
            start=True, stop=True))(),
            reads=[r_ones_rf, r_b1rowf], writes=[rp[3]], tag="b1mbc")
        for t in range(4):
            P.op("vector", (lambda h=h, t=t: lambda e: e.tensor_copy(
                b1mB4[h][:, t * 128:(t + 1) * 128], p3[:, 0:128]))(),
                reads=[rp[3]], writes=[r_b1mB4[h]], tag="b1mbc_ev")

    def cast_load(dst, dstv, src, res, n3):
        P.op("gpsimd", lambda e: e.dma_start(out=dstv, in_=src), writes=[res],
             dma=True, tag="wload")

    wq_v = wq_b[:, :].rearrange("p (c m) -> p c m", c=NCH)
    wk_v = wk_b[:, :].rearrange("p (c m) -> p c m", c=NCH)
    wv_v = wv_b[:, :].rearrange("p (c m) -> p c m", c=NCH)
    cast_load(wq_b, wq_v, wq.rearrange("(c p) m -> p c m", p=128), r_wq, NCH)
    cast_load(wk_b, wk_v, wk.rearrange("(c p) m -> p c m", p=128), r_wk, NCH)
    cast_load(wv_b, wv_v, wv.rearrange("(c p) m -> p c m", p=128), r_wv, NCH)
    wo_v = wo_b[:, :].rearrange("p (c m) -> p c m", c=NCH)

    def wo_load():
        P.op("gpsimd", lambda e: e.dma_start(
            out=wo_v, in_=wo.rearrange("(c p) m -> p c m", p=128)),
            writes=[r_wo], dma=True, tag="wo_load")

    for wi, (wbuf, res) in enumerate(((wq_b, r_wq), (wk_b, r_wk), (wv_b, r_wv))):
        W = HPC * DK
        for c in range(NCH):
            P.op("tensor", (lambda wbuf=wbuf, c=c, W=W: lambda e: e.matmul(
                p0[0:1, 0:W], ones_c[:, :], wbuf[:, c * W:(c + 1) * W],
                start=(c == 0), stop=(c == NCH - 1)))(),
                reads=[res, r_ones_c], writes=[rp[0]], tag="wsum_mm")
        P.op("scalar", (lambda wi=wi, W=W: lambda e: e.mul(
            wsum[0:1, wi * W:(wi + 1) * W], p0[0:1, 0:W], -1.0))(),
            reads=[rp[0]], writes=[r_wsum], tag="wsum_evac")

    for h in range(HPC):
        P.op("vector", (lambda h=h: lambda e: e.memset(memz_f[h][:, 0:DK], 0.0))(),
             writes=[r_memz_f[h]])
        P.op("vector", (lambda h=h: lambda e: e.memset(
            memz_f[h][:, DK:DK + 1], 1.0 / DK))(),
            reads=[r_memz_f[h]], writes=[r_memz_f[h]])
        P.op("scalar", (lambda h=h: lambda e: e.copy(
            memz_b[h][:, :], memz_f[h][:, :]))(),
            reads=[r_memz_f[h]], writes=[r_memz_b[h]])
        P.op("vector", (lambda h=h: lambda e: e.tensor_mul(
            memb_beta[h][:, :], memz_f[h][:, :], betaB[h][:, :]))(),
            reads=[r_memz_f[h], r_betaB[h]], writes=[r_membeta[h]],
            tag="membeta0")

    xbf = [sba(f"xbf{i}", [128, NCH * SEG], BF16) for i in range(2)]
    r_xbf = [[R(f"xbf{i}c{c}") for c in range(NCH)] for i in range(2)]
    x2b = [sba(f"x2b{i}", [128, 512], BF16) for i in range(2)]
    r_x2b = [R(f"x2b{i}") for i in range(2)]
    rowf = sba("rowf", [1, 1024], F32)
    r_rowf = R("rowf", True)
    rows_b = [sba(f"rows_b{i}", [1, 1024], BF16) for i in range(2)]
    r_rows_b = [R(f"rows_b{i}", True) for i in range(2)]
    rbc = [sba(f"rbc{i}", [128, 512], F32) for i in range(2)]
    r_rbc = [R(f"rbc{i}") for i in range(2)]

    def _bufs2(nm, w=512, dt=BF16):
        return ([[sba(f"{nm}{p}{h}", [128, w], dt) for h in range(HPC)]
                 for p in range(2)],
                [[R(f"{nm}{p}{h}") for h in range(HPC)] for p in range(2)])

    qS, r_qS = _bufs2("qS")
    kT, r_kT = _bufs2("kT")
    sq, r_sq = _bufs2("sq")
    skT, r_skT = _bufs2("skT")
    skt, r_skt = _bufs2("skt")
    vtk, r_vtk = _bufs2("vtk")
    vts, r_vts = _bufs2("vts")

    def _bufs1(nm, w=512, dt=BF16, strict=False):
        return ([sba(f"{nm}{h}", [128, w], dt) for h in range(HPC)],
                [R(f"{nm}{h}", strict) for h in range(HPC)])

    vdv, r_vdv = _bufs1("vdv")
    dlt, r_dlt = _bufs1("dlt", 4 * (DV + 1))
    sg1 = sba("sg1", [128, 512], BF16)
    sg2 = sba("sg2", [128, 512], BF16)
    r_sg1, r_sg2 = R("sg1"), R("sg2")
    Pb = [sba(f"Pb{h}", [128, 4 * 512], BF16) for h in range(HPC)]
    r_Pb = [[R(f"Pb{h}c{tc}") for tc in range(4)] for h in range(HPC)]
    attq, r_attq = _bufs1("attq")
    attt, r_attt = _bufs1("attt")
    tmpc, r_tmpc = _bufs1("tmpc", 128, F32)
    zrot = sba("zrot", [128, 128], F32)
    r_zrot = R("zrot")
    P.op("vector", lambda e: e.memset(zrot[:, :], 0.0), writes=[r_zrot])
    rds, r_rds = _bufs1("rds", 8, F32, True)
    rnd, r_rnd = _bufs1("rnd", 4, F32, True)
    lnrow = sba("lnrow", [1, 512], F32)
    r_lnrow = R("lnrow", True)
    attP = [sba(f"attP{i}", [128, 4 * 512], BF16) for i in range(2)]
    r_attP = [R(f"attP{i}") for i in range(2)]
    outb = sba("outb", [128, NCH * 64], F32)
    r_outb = R("outb")

    for h in range(HPC):
        P.op("vector", (lambda h=h: lambda e: e.memset(
            dlt[h][:, :], -1.0))(), writes=[r_dlt[h]])

    def seg_dma(g):
        b = g % 2
        src = xT[:, g * SEG:(g + 1) * SEG].rearrange("(c p) s -> p c s", p=128)
        dst = xbf[b][:, :].rearrange("p (c s) -> p c s", s=SEG)
        P.op("gpsimd", (lambda src=src, dst=dst: lambda e: e.dma_start(
            out=dst, in_=src))(), reads=[r_xT], writes=r_xbf[b], dma=True,
            tag=f"xdma{g}")

    def stats_mm(g):
        b = g % 2
        for c in range(NCH):
            P.op("vector", (lambda b=b, c=c: lambda e: e.tensor_mul(
                x2b[c % 2][:, :], xbf[b][:, c * SEG:c * SEG + 512],
                xbf[b][:, c * SEG:c * SEG + 512]))(),
                reads=[r_xbf[b][c]], writes=[r_x2b[c % 2]], tag="x2")
            P.op("tensor", (lambda b=b, c=c: lambda e: e.matmul(
                p0[64:65, 0:512], ones_c[:, :], xbf[b][:, c * SEG:c * SEG + 512],
                start=(c == 0), stop=(c == NCH - 1)))(),
                reads=[r_xbf[b][c], r_ones_c], writes=[rp[0]], tag="mmM")
            P.op("tensor", (lambda c=c: lambda e: e.matmul(
                p1[64:65, 0:512], ones_c[:, :], x2b[c % 2][:, :],
                start=(c == 0), stop=(c == NCH - 1)))(),
                reads=[r_x2b[c % 2], r_ones_c], writes=[rp[1]], tag="mmS")

    def stats_evac(g):
        b = g % 2
        rf, rb = rowf, rows_b[b]
        rrf, rrb = r_rowf, r_rows_b[b]
        P.op("vector", (lambda rf=rf: lambda e: e.tensor_scalar_mul(
            rf[0:1, 0:512], p0[64:65, 0:512], 1.0 / D))(),
            reads=[rp[0]], writes=[rrf], tag="rowm")
        P.op("vector", (lambda rf=rf: lambda e: e.tensor_scalar_mul(
            rf[0:1, 512:1024], p1[64:65, 0:512], 1.0 / D))(),
            reads=[rp[1]], writes=[rrf], tag="rowsq")
        P.op("scalar", (lambda rf=rf, rb=rb: lambda e: e.copy(
            rb[0:1, 512:1024], rf[0:1, 0:512]))(),
            reads=[rrf], writes=[rrb], tag="mrow")
        P.op("scalar", (lambda rf=rf: lambda e: e.activation(
            lnrow[0:1, :], rf[0:1, 0:512], AF.Square))(),
            reads=[rrf], writes=[r_lnrow], tag="msq")
        P.op("vector", (lambda rf=rf: lambda e: e.tensor_sub(
            rf[0:1, 512:1024], rf[0:1, 512:1024], lnrow[0:1, :]))(),
            reads=[rrf, r_lnrow], writes=[rrf], tag="var")
        P.op("scalar", (lambda rf=rf: lambda e: e.activation(
            rf[0:1, 512:1024], rf[0:1, 512:1024], AF.Ln,
            bias=epsc[0:1, 0:1]))(),
            reads=[rrf, r_epsc], writes=[rrf], tag="ln")
        P.op("scalar", (lambda rf=rf, rb=rb: lambda e: e.activation(
            rb[0:1, 0:512], rf[0:1, 512:1024], AF.Exp, scale=-0.5))(),
            reads=[rrf], writes=[rrb], tag="exp")

    def stats_rows(g):
        b = g % 2
        P.op("tensor", (lambda b=b: lambda e: e.matmul(
            p6[:, :], ones_r[:, :], rows_b[b][0:1, 0:512],
            start=True, stop=True))(),
            reads=[r_rows_b[b], r_ones_r], writes=[rp[6]], tag="rbc_mm")
        P.op("vector", (lambda b=b: lambda e: e.tensor_copy(
            rbc[b][:, :], p6[:, :]))(),
            reads=[rp[6]], writes=[r_rbc[b]], tag="rbc_evac")

    def proj(g, h, wbuf, r_wbuf, wslot, wsoff, dst, r_dst, pbank, rb_,
             scl=None):
        b = g % 2
        W = HPC * DK
        for c in range(NCH):
            P.op("tensor", (lambda c=c, wbuf=wbuf, wslot=wslot, b=b,
                            pbank=pbank: lambda e:
                 e.matmul(pbank[:, :], wbuf[:, c * W + wslot:c * W + wslot + 128],
                          xbf[b][:, c * SEG:c * SEG + 512],
                          start=(c == 0), stop=False))(),
                reads=[r_wbuf, r_xbf[b][c]], writes=[rb_], tag="prj")
        P.op("tensor", (lambda wsoff=wsoff, wslot=wslot, b=b,
                        pbank=pbank: lambda e: e.matmul(
            pbank[:, :], wsum[0:1, wsoff + wslot:wsoff + wslot + 128],
            rows_b[b][0:1, 512:1024], start=False, stop=True))(),
            reads=[r_wsum, r_rows_b[b]], writes=[rb_], tag="rank1")
        if scl is None:
            P.op("vector", (lambda dst=dst, b=b, pbank=pbank: lambda e:
                 e.tensor_mul(dst[:, :], pbank[:, :], rbc[b][:, :]))(),
                reads=[rb_, r_rbc[b]], writes=[r_dst], tag="corr1")
        else:
            P.op("vector", (lambda dst=dst, b=b, scl=scl, pbank=pbank: lambda e:
                 e.scalar_tensor_tensor(dst[:, :], pbank[:, :], scl, rbc[b][:, :],
                                        ALU.mult, ALU.mult))(),
                reads=[rb_, r_rbc[b]], writes=[r_dst], tag="corr1")

    def sigma(src, r_src, dst, r_dst, comp=1.0):
        P.op("scalar", (lambda src=src: lambda e: e.activation(
            sg1[:, :], src[:, :], AF.Relu, scale=-1.0))(),
            reads=[r_src], writes=[r_sg1], tag="sgrelu1")
        P.op("scalar", (lambda comp=comp: lambda e: e.activation(
            sg2[:, :], sg1[:, :], AF.Exp, scale=-comp))(),
            reads=[r_sg1], writes=[r_sg2], tag="sgexp")
        P.op("scalar", (lambda src=src: lambda e: e.activation(
            sg1[:, :], src[:, :], AF.Relu))(),
            reads=[r_src], writes=[r_sg1], tag="sgrelu2")
        if comp == 1.0:
            P.op("vector", (lambda dst=dst: lambda e: e.tensor_add(
                dst[:, :], sg1[:, :], sg2[:, :]))(),
                reads=[r_sg1, r_sg2], writes=[r_dst], tag="sgadd")
        else:
            P.op("vector", (lambda dst=dst, comp=comp: lambda e:
                 e.scalar_tensor_tensor(dst[:, :], sg1[:, :], comp, sg2[:, :],
                                        ALU.mult, ALU.add))(),
                reads=[r_sg1, r_sg2], writes=[r_dst], tag="sgadd")

    def transp4(src, r_src, dst, r_dst):
        for t in range(4):
            P.op("tensor", (lambda t=t, src=src: lambda e: e.transpose(
                p7b[:, t * 128:(t + 1) * 128], src[:, t * 128:(t + 1) * 128],
                ident[:, :]))(),
                reads=[r_src, r_ident], writes=[r_p7s], tag="tr4")
        P.op("vector", (lambda dst=dst: lambda e: e.tensor_copy(
            dst[:, :], p7b[:, 0:512]))(),
            reads=[r_p7s], writes=[r_dst], tag="tr4_evac")

    def qkvA(g, h):
        b = g % 2
        ws = h * DK
        W = HPC * DK
        bk = [(p2, rp[2]), (p3, rp[3])]
        proj(g, h, wq_b, r_wq, ws, 0, qS[b][h], r_qS[b][h],
             *bk[(h * 3 + 0) % 2], scl=SCALE)
        proj(g, h, wk_b, r_wk, ws, W, kT[b][h], r_kT[b][h],
             *bk[(h * 3 + 1) % 2])
        proj(g, h, wv_b, r_wv, ws, 2 * W, vdv[h], r_vdv[h],
             *bk[(h * 3 + 2) % 2])

    def qkvSig(g, h):
        b = g % 2
        sigma(qS[b][h], r_qS[b][h], sq[b][h], r_sq[b][h], comp=math.sqrt(DK))
        sigma(kT[b][h], r_kT[b][h], skT[b][h], r_skT[b][h])

    def qkvTr(g, h):
        b = g % 2
        transp4(vdv[h], r_vdv[h], vtk[b][h], r_vtk[b][h])
        P.op("vector", (lambda b=b, h=h: lambda e: e.tensor_mul(
            vts[b][h][:, :], vtk[b][h][:, :], b1mB4[h][:, :]))(),
            reads=[r_vtk[b][h], r_b1mB4[h]], writes=[r_vts[b][h]], tag="vts")
        transp4(skT[b][h], r_skT[b][h], skt[b][h], r_skt[b][h])

    def sftT(g, h):
        b = g % 2
        for tc in range(4):
            n = 512 - tc * 128
            pbank, rb_ = (p0, rp[0]) if tc % 2 == 0 else (p1, rp[1])
            P.op("tensor", (lambda tc=tc, h=h, b=b, n=n, pbank=pbank: lambda e:
                 e.matmul(pbank[:, 0:n], kT[b][h][:, tc * 128:(tc + 1) * 128],
                          qS[b][h][:, tc * 128:512], start=True, stop=True))(),
                reads=[r_kT[b][h], r_qS[b][h]], writes=[rb_], tag="scoreT")
            P.op("scalar", (lambda tc=tc, h=h, n=n, pbank=pbank: lambda e:
                 e.activation(Pb[h][:, tc * 512 + tc * 128:tc * 512 + 512],
                              pbank[:, 0:n], AF.Exp, bias=negC[:, 0:1]))(),
                reads=[rb_, r_negC], writes=[r_Pb[h][tc]], tag="expT")
            P.op("vector", (lambda tc=tc, h=h: lambda e: e.tensor_mul(
                Pb[h][:, tc * 512 + tc * 128:tc * 512 + tc * 128 + 128],
                Pb[h][:, tc * 512 + tc * 128:tc * 512 + tc * 128 + 128],
                trimask[:, :]))(),
                reads=[r_Pb[h][tc], r_trimask], writes=[r_Pb[h][tc]],
                tag="trimul")

    def memA(g, h):
        b = g % 2
        pM, rM = (p4, rp[4]) if h == 0 else (p5, rp[5])
        base = 384 + h * 16
        for sc in range(4):
            P.op("tensor", (lambda sc=sc, h=h, b=b, pM=pM: lambda e: e.matmul(
                pM[:, sc * 128:(sc + 1) * 128],
                sq[b][h][:, sc * 128:(sc + 1) * 128],
                memb_beta[h][:, 0:DK], start=True, stop=True))(),
                reads=[r_sq[b][h], r_membeta[h]], writes=[rM], tag="attM")
        for sc in range(4):
            P.op("tensor", (lambda sc=sc, h=h, b=b, base=base: lambda e:
                 e.matmul(p7[:, base + sc:base + sc + 1],
                          sq[b][h][:, sc * 128:(sc + 1) * 128],
                          memb_beta[h][:, DK:DK + 1], start=True, stop=True))(),
                reads=[r_sq[b][h], r_membeta[h]], writes=[r_p7t[h]], tag="attD")

    def dotB(g, h):
        b = g % 2
        base = 384 + h * 16
        for sc in range(4):
            for tc in range(sc + 1):
                P.op("tensor", (lambda sc=sc, tc=tc, h=h, b=b: lambda e:
                     e.matmul(p6[:, sc * 128:(sc + 1) * 128],
                              Pb[h][:, tc * 512 + sc * 128:
                                   tc * 512 + (sc + 1) * 128],
                              vts[b][h][:, tc * 128:(tc + 1) * 128],
                              start=(tc == 0), stop=(tc == sc)))(),
                    reads=[r_Pb[h][tc], r_vts[b][h]], writes=[rp[6]],
                    tag="attdot")
                P.op("tensor", (lambda sc=sc, tc=tc, h=h, base=base: lambda e:
                     e.matmul(p7[:, base + 4 + sc:base + 5 + sc],
                              Pb[h][:, tc * 512 + sc * 128:
                                   tc * 512 + (sc + 1) * 128],
                              ones_c[:, :], start=(tc == 0), stop=(tc == sc)))(),
                    reads=[r_Pb[h][tc], r_ones_c], writes=[r_p7t[h]],
                    tag="sume")
        P.op("vector", (lambda h=h, base=base: lambda e: e.reciprocal(
            rds[h][:, 0:8], p7[:, base:base + 8]))(),
            reads=[r_p7t[h]], writes=[r_rds[h]], tag="rcp_ds")

    def cmbT(g, h):
        pM, rM = (p4, rp[4]) if h == 0 else (p5, rp[5])
        for sc in range(4):
            P.op("vector", (lambda sc=sc, h=h: lambda e:
                 e.scalar_tensor_tensor(
                tmpc[h][:, :], p6[:, sc * 128:(sc + 1) * 128],
                rds[h][:, 4 + sc:5 + sc], zrot[:, :], ALU.mult, ALU.add))(),
                reads=[rp[6], r_rds[h], r_zrot], writes=[r_tmpc[h]], tag="cmbA")
            P.op("vector", (lambda sc=sc, h=h, pM=pM: lambda e:
                 e.scalar_tensor_tensor(
                attq[h][:, sc * 128:(sc + 1) * 128],
                pM[:, sc * 128:(sc + 1) * 128], rds[h][:, sc:sc + 1],
                tmpc[h][:, :], ALU.mult, ALU.add))(),
                reads=[rM, r_rds[h], r_tmpc[h]], writes=[r_attq[h]], tag="cmbB")
        for t in range(4):
            P.op("tensor", (lambda t=t, h=h: lambda e: e.transpose(
                p7b[:, t * 128:(t + 1) * 128], attq[h][:, t * 128:(t + 1) * 128],
                ident[:, :]))(),
                reads=[r_attq[h], r_ident], writes=[r_p7s], tag="attT")
        P.op("vector", (lambda h=h: lambda e: e.tensor_copy(
            attt[h][:, :], p7b[:, 0:512]))(),
            reads=[r_p7s], writes=[r_attt[h]], tag="attT_ev")
        pp = (g // 2) % 2
        blk = (g % 2) * 2 + h
        dstv = attP[pp][:, blk * 512:(blk + 1) * 512].rearrange(
            "p (c a) -> p a c", a=32)
        srcv = attt[h][:, :].rearrange("p (a c) -> p a c", c=16)
        P.op("scalar", (lambda dstv=dstv, srcv=srcv: lambda e: e.copy(
            dstv, srcv))(),
            reads=[r_attt[h]], writes=[r_attP[pp]], tag="attT_pk")

    def deltaC(g, h):
        b = g % 2
        pM, rM = (p4, rp[4]) if h == 0 else (p5, rp[5])
        base = 384 + h * 16
        for sc in range(4):
            P.op("tensor", (lambda sc=sc, h=h, b=b, pM=pM: lambda e: e.matmul(
                pM[:, sc * 128:(sc + 1) * 128],
                skT[b][h][:, sc * 128:(sc + 1) * 128], memz_b[h][:, 0:DK],
                start=True, stop=True))(),
                reads=[r_skT[b][h], r_memz_b[h]], writes=[rM], tag="nk")
        for sc in range(4):
            P.op("tensor", (lambda sc=sc, h=h, b=b, base=base: lambda e:
                 e.matmul(p7[:, base + 8 + sc:base + 9 + sc],
                          skT[b][h][:, sc * 128:(sc + 1) * 128],
                          memz_b[h][:, DK:DK + 1], start=True, stop=True))(),
                reads=[r_skT[b][h], r_memz_b[h]], writes=[r_p7t[h]], tag="nd")
        P.op("vector", (lambda h=h, base=base: lambda e: e.reciprocal(
            rnd[h][:, :], p7[:, base + 8:base + 12]))(),
            reads=[r_p7t[h]], writes=[r_rnd[h]], tag="rcp_nd")
        for sc in range(4):
            P.op("vector", (lambda sc=sc, h=h, b=b, pM=pM: lambda e:
                 e.scalar_tensor_tensor(
                dlt[h][:, sc * 129:sc * 129 + 128],
                pM[:, sc * 128:(sc + 1) * 128], rnd[h][:, sc:sc + 1],
                vtk[b][h][:, sc * 128:(sc + 1) * 128],
                ALU.mult, ALU.subtract))(),
                reads=[rM, r_rnd[h], r_vtk[b][h]], writes=[r_dlt[h]],
                tag="delta")
        for sc in range(4):
            P.op("tensor", (lambda sc=sc, h=h, b=b, pM=pM: lambda e: e.matmul(
                pM[:, 0:129], skt[b][h][:, sc * 128:(sc + 1) * 128],
                dlt[h][:, sc * 129:sc * 129 + 129],
                start=(sc == 0), stop=(sc == 3)))(),
                reads=[r_skt[b][h], r_dlt[h]], writes=[rM], tag="upd")
        P.op("vector", (lambda h=h, pM=pM: lambda e: e.tensor_sub(
            memz_f[h][:, :], memz_f[h][:, :], pM[:, 0:129]))(),
            reads=[rM, r_memz_f[h]], writes=[r_memz_f[h]], tag="memupd")
        P.op("scalar", (lambda h=h: lambda e: e.copy(
            memz_b[h][:, :], memz_f[h][:, :]))(),
            reads=[r_memz_f[h]], writes=[r_memz_b[h]], tag="memcast")
        P.op("vector", (lambda h=h: lambda e: e.tensor_mul(
            memb_beta[h][:, :], memz_f[h][:, :], betaB[h][:, :]))(),
            reads=[r_memz_f[h], r_betaB[h]], writes=[r_membeta[h]],
            tag="membeta")

    def wo_pair(p_idx):
        pp = p_idx % 2
        attv = attP[pp][:, :].rearrange("p (b s) -> p b s", s=512)
        for half in range(2):
            for eg in range(2):
                pbank, rb_ = (p0, rp[0]) if eg % 2 == 0 else (p1, rp[1])
                for ei in range(4):
                    e_ch = half * 8 + eg * 4 + ei
                    for j in range(NCH):
                        P.op("tensor", (lambda e_ch=e_ch, ei=ei, j=j,
                                        pbank=pbank, attv=attv: lambda e:
                             e.matmul(
                            pbank[:, ei * 128:(ei + 1) * 128],
                            wo_b[:, j * D + e_ch * 128:j * D + (e_ch + 1) * 128],
                            attv[:, :, j * 32:(j + 1) * 32],
                            start=(j == 0), stop=(j == NCH - 1)))(),
                            reads=[r_wo, r_attP[pp]], writes=[rb_], tag="womm")
                P.op("vector", (lambda eg=eg, pbank=pbank: lambda e:
                     e.tensor_copy(
                    outb[:, eg * 512:(eg + 1) * 512], pbank[:, :]))(),
                    reads=[rb_], writes=[r_outb], tag="woev")
            dst = att_scr[half * 1024:(half + 1) * 1024,
                          p_idx * 128:(p_idx + 1) * 128].rearrange(
                "(c p) r -> p c r", p=128)
            srcv = outb[:, :].rearrange("p (c r) -> p c r", r=128)
            P.op("sync", (lambda dst=dst, srcv=srcv: lambda e: e.dma_start(
                out=dst, in_=srcv))(), reads=[r_outb], writes=[r_scr],
                dma=True, tag="scrdma")

    if nseg > 0:
        seg_dma(0)
        stats_mm(0)
        stats_evac(0)
        stats_rows(0)
        qkvA(0, 0)
        qkvA(0, 1)
        seg_dma(1)
        wo_load()
        qkvSig(0, 0)
        qkvSig(0, 1)
        sftT(0, 0)
        sftT(0, 1)
        qkvTr(0, 0)
        qkvTr(0, 1)
    for g in range(nseg):
        nxt = g + 1 < nseg
        if nxt:
            stats_mm(g + 1)
        memA(g, 0)
        dotB(g, 0)
        if nxt:
            stats_evac(g + 1)
        cmbT(g, 0)
        memA(g, 1)
        dotB(g, 1)
        cmbT(g, 1)
        deltaC(g, 0)
        if nxt:
            stats_rows(g + 1)
            qkvA(g + 1, 0)
            qkvSig(g + 1, 0)
            sftT(g + 1, 0)
        deltaC(g, 1)
        if nxt:
            qkvA(g + 1, 1)
            qkvSig(g + 1, 1)
            sftT(g + 1, 1)
            qkvTr(g + 1, 0)
            qkvTr(g + 1, 1)
        if g % 2 == 1:
            wo_pair(g // 2)
        if g + 2 < nseg:
            seg_dma(g + 2)

    P.fence()
    esa.close()
    xres = sb("xres", [128, NCH * ROWS], F32)
    xln = sb("xln", [128, NCH * ROWS], BF16)
    hbf = sb("hbf", [128, 8 * ROWS], BF16)
    QSUP = SUPI // 4
    wgg = [sb(f"wgg{i}", [128, NCH * QSUP], BF16) for i in range(2)]
    wgu = [sb(f"wgu{i}", [128, NCH * QSUP], BF16) for i in range(2)]
    wdb = sb("wdb", [128, 8 * D], BF16)
    sgf = sb("sgf", [128, 2048], F32)
    x2f = sb("x2f", [128, 1024], BF16)
    rowF = sb("rowF", [1, 2048], F32)
    rowFb = sb("rowFb", [1, 2048], BF16)
    rbc2 = sb("rbc2", [128, 1024], F32)
    mbc2 = sb("mbc2", [128, 1024], F32)
    r_xres, r_xln, r_hbf = R("xres"), R("xln"), R("hbf")
    r_wgg = [R(f"wgg{i}") for i in range(2)]
    r_wgu = [R(f"wgu{i}") for i in range(2)]
    r_wdb = R("wdb")
    r_sgf, r_x2f = R("sgf"), R("x2f")
    r_rowF, r_rowFb = R("rowF", True), R("rowFb", True)
    r_rbc2, r_mbc2 = R("rbc2"), R("mbc2")

    r_xresg = [R(f"xresg{i}") for i in range(4)]
    for grp in range(4):
        cs = slice(grp * 4 * ROWS, (grp + 1) * 4 * ROWS)
        P.op("sync", (lambda grp=grp, cs=cs: lambda e: e.dma_start(
            out=xres[:, cs].rearrange("p (c s) -> p c s", s=ROWS),
            in_=xrt[grp * 512:(grp + 1) * 512, :].rearrange(
                "(c p) s -> p c s", p=128)))(),
            writes=[r_xresg[grp]], dma=True, tag="xrt_dma")
        P.op("gpsimd", (lambda grp=grp, cs=cs: lambda e: e.dma_start(
            out=xres[:, cs].rearrange("p (c s) -> p c s", s=ROWS),
            in_=att_scr[grp * 512:(grp + 1) * 512, :].rearrange(
                "(c p) s -> p c s", p=128), accum_op=ALU.add))(),
            reads=[r_scr, r_xresg[grp]], writes=[r_xresg[grp]], dma=True,
            tag="scr_acc")
        for c in range(grp * 4, grp * 4 + 4):
            P.op("scalar", (lambda c=c, grp=grp: lambda e: e.copy(
                xln[:, c * ROWS:(c + 1) * ROWS],
                xres[:, c * ROWS:(c + 1) * ROWS]))(),
                reads=[r_xresg[grp]], writes=[r_xln], tag="xcast")
    P.op("vector", lambda e: e.nop(),
         reads=[r_xresg[0], r_xresg[1], r_xresg[2], r_xresg[3]],
         writes=[r_xres], tag="xres_join")
    for c in range(NCH):
        P.op("scalar", (lambda c=c: lambda e: e.activation(
            x2f[:, :], xln[:, c * ROWS:(c + 1) * ROWS], AF.Square))(),
            reads=[r_xln], writes=[r_x2f], tag="fx2")
        for sh in range(2):
            P.op("tensor", (lambda c=c, sh=sh: lambda e: e.matmul(
                p0[0:1, 0:512] if sh == 0 else p2[0:1, 0:512], ones_c[:, :],
                xln[:, c * ROWS + sh * 512:c * ROWS + sh * 512 + 512],
                start=(c == 0), stop=(c == NCH - 1)))(),
                reads=[r_xln, r_ones_c], writes=[rp[0] if sh == 0 else rp[2]],
                tag="fmmM")
            P.op("tensor", (lambda c=c, sh=sh: lambda e: e.matmul(
                p1[0:1, 0:512] if sh == 0 else p3[0:1, 0:512], ones_c[:, :],
                x2f[:, sh * 512:sh * 512 + 512],
                start=(c == 0), stop=(c == NCH - 1)))(),
                reads=[r_x2f, r_ones_c], writes=[rp[1] if sh == 0 else rp[3]],
                tag="fmmS")
    for sh, (pm, pv, rm, rv) in enumerate(((p0, p1, rp[0], rp[1]),
                                           (p2, p3, rp[2], rp[3]))):
        o = sh * 512
        P.op("vector", (lambda o=o, pm=pm: lambda e: e.tensor_scalar_mul(
            rowF[0:1, o:o + 512], pm[0:1, 0:512], 1.0 / D))(),
            reads=[rm], writes=[r_rowF], tag="frowm")
        P.op("vector", (lambda o=o: lambda e: e.tensor_mul(
            rowF[0:1, 1024 + o:1024 + o + 512], rowF[0:1, o:o + 512],
            rowF[0:1, o:o + 512]))(),
            reads=[r_rowF], writes=[r_rowF], tag="fmsq")
        P.op("vector", (lambda o=o, pv=pv: lambda e: e.scalar_tensor_tensor(
            rowF[0:1, 1024 + o:1024 + o + 512], pv[0:1, 0:512], 1.0 / D,
            rowF[0:1, 1024 + o:1024 + o + 512], ALU.mult, ALU.subtract))(),
            reads=[rv, r_rowF], writes=[r_rowF], tag="fvar")
    P.op("scalar", lambda e: e.activation(rowF[0:1, 1024:2048],
                                          rowF[0:1, 1024:2048], AF.Ln,
                                          bias=epsc[0:1, 0:1]),
         reads=[r_rowF, r_epsc], writes=[r_rowF], tag="fln")
    P.op("scalar", lambda e: e.activation(rowF[0:1, 1024:2048],
                                          rowF[0:1, 1024:2048], AF.Exp,
                                          scale=-0.5),
         reads=[r_rowF], writes=[r_rowF], tag="fexp")
    P.op("vector", lambda e: e.tensor_mul(rowF[0:1, 0:1024], rowF[0:1, 0:1024],
                                          rowF[0:1, 1024:2048]),
         reads=[r_rowF], writes=[r_rowF], tag="fmr")
    P.op("scalar", lambda e: e.copy(rowFb[0:1, :], rowF[0:1, :]),
         reads=[r_rowF], writes=[r_rowFb], tag="frowb")
    for sh in range(2):
        P.op("tensor", (lambda sh=sh: lambda e: e.matmul(
            p5[:, 0:512], ones_r[:, :],
            rowFb[0:1, 1024 + sh * 512:1024 + sh * 512 + 512],
            start=True, stop=True))(),
            reads=[r_rowFb, r_ones_r], writes=[rp[5]], tag="frbc")
        P.op("vector", (lambda sh=sh: lambda e: e.tensor_copy(
            rbc2[:, sh * 512:sh * 512 + 512], p5[:, 0:512]))(),
            reads=[rp[5]], writes=[r_rbc2], tag="frbc_ev")
        P.op("tensor", (lambda sh=sh: lambda e: e.matmul(
            p6[:, 0:512], ones_r[:, :],
            rowFb[0:1, sh * 512:sh * 512 + 512], start=True, stop=True))(),
            reads=[r_rowFb, r_ones_r], writes=[rp[6]], tag="fmbc")
        P.op("vector", (lambda sh=sh: lambda e: e.tensor_copy(
            mbc2[:, sh * 512:sh * 512 + 512], p6[:, 0:512]))(),
            reads=[rp[6]], writes=[r_mbc2], tag="fmbc_ev")
    for c in range(NCH):
        P.op("vector", (lambda c=c: lambda e: e.tensor_mul(
            sgf[:, 0:1024], xres[:, c * ROWS:(c + 1) * ROWS], rbc2[:, :]))(),
            reads=[r_xres, r_rbc2], writes=[r_sgf], tag="fnorm1")
        P.op("vector", (lambda c=c: lambda e: e.tensor_sub(
            xln[:, c * ROWS:(c + 1) * ROWS], sgf[:, 0:1024], mbc2[:, :]))(),
            reads=[r_sgf, r_mbc2], writes=[r_xln], tag="fnorm2")

    def gate_dma(sup, quarter):
        sl = quarter % 2
        o0 = sup * SUPI + quarter * QSUP
        gsrc = wgut[:, o0:o0 + QSUP].rearrange("(c p) o -> p c o", p=128)
        usrc = wgut[:, 8192 + o0:8192 + o0 + QSUP].rearrange(
            "(c p) o -> p c o", p=128)
        P.op("gpsimd", (lambda gsrc=gsrc, sl=sl: lambda e: e.dma_start(
            out=wgg[sl][:, :].rearrange("p (c o) -> p c o", o=QSUP),
            in_=gsrc))(), writes=[r_wgg[sl]], dma=True, tag="wgg_dma")
        P.op("gpsimd", (lambda usrc=usrc, sl=sl: lambda e: e.dma_start(
            out=wgu[sl][:, :].rearrange("p (c o) -> p c o", o=QSUP),
            in_=usrc))(), writes=[r_wgu[sl]], dma=True, tag="wgu_dma")

    def wd_dma(sup):
        wsrc = wdt[sup * SUPI:(sup + 1) * SUPI, :].rearrange(
            "(b p) h2 -> p b h2", p=128)
        P.op("gpsimd", (lambda wsrc=wsrc: lambda e: e.dma_start(
            out=wdb[:, :].rearrange("p (b h2) -> p b h2", h2=D), in_=wsrc))(),
            writes=[r_wdb], dma=True, tag="wd_dma")

    gate_dma(0, 0)
    gate_dma(0, 1)
    wd_dma(0)
    for sup in range(nsup):
        for quarter in range(4):
            sl = quarter % 2
            for jj in range(2):
                j = quarter * 2 + jj
                for sh in range(2):
                    k4 = (j % 2) * 4
                    pg, rg = psums[k4 + 2 * sh], rp[k4 + 2 * sh]
                    pu, ru = psums[k4 + 2 * sh + 1], rp[k4 + 2 * sh + 1]
                    for c in range(NCH):
                        P.op("tensor", (lambda c=c, jj=jj, sh=sh, pg=pg, sl=sl:
                             lambda e: e.matmul(
                            pg[:, :],
                            wgg[sl][:, c * QSUP + jj * 128:c * QSUP + (jj + 1) * 128],
                            xln[:, c * ROWS + sh * 512:c * ROWS + sh * 512 + 512],
                            start=(c == 0), stop=(c == NCH - 1)))(),
                            reads=[r_wgg[sl], r_xln], writes=[rg], tag="gmm")
                        P.op("tensor", (lambda c=c, jj=jj, sh=sh, pu=pu, sl=sl:
                             lambda e: e.matmul(
                            pu[:, :],
                            wgu[sl][:, c * QSUP + jj * 128:c * QSUP + (jj + 1) * 128],
                            xln[:, c * ROWS + sh * 512:c * ROWS + sh * 512 + 512],
                            start=(c == 0), stop=(c == NCH - 1)))(),
                            reads=[r_wgu[sl], r_xln], writes=[ru], tag="umm")
                    P.op("scalar", (lambda sh=sh, pg=pg, j=j: lambda e:
                         e.activation(
                        sgf[:, (j % 2) * 1024 + sh * 512:
                            (j % 2) * 1024 + sh * 512 + 512],
                        pg[:, :], AF.Silu))(),
                        reads=[rg], writes=[r_sgf], tag="silu")
                    P.op("vector", (lambda j=j, sh=sh, pu=pu: lambda e:
                         e.tensor_mul(
                        hbf[:, j * ROWS + sh * 512:j * ROWS + sh * 512 + 512],
                        sgf[:, (j % 2) * 1024 + sh * 512:
                            (j % 2) * 1024 + sh * 512 + 512], pu[:, :]))(),
                        reads=[r_sgf, ru], writes=[r_hbf], tag="hmul")
            qn = sup * 4 + quarter + 2
            if qn < nsup * 4:
                gate_dma(qn // 4, qn % 4)

        for eg in range(8):
            for ei in range(2):
                e_ch = eg * 2 + ei
                for sh in range(2):
                    pd, rd_ = ((p4, rp[4]), (p5, rp[5]),
                               (p6, rp[6]), (p7, rp[7]))[ei * 2 + sh]
                    for b in range(8):
                        P.op("tensor", (lambda e_ch=e_ch, sh=sh, b=b, pd=pd:
                             lambda e: e.matmul(
                            pd[:, :],
                            wdb[:, b * D + e_ch * 128:b * D + (e_ch + 1) * 128],
                            hbf[:, b * ROWS + sh * 512:b * ROWS + sh * 512 + 512],
                            start=(b == 0), stop=(b == 7)))(),
                            reads=[r_wdb, r_hbf], writes=[rd_], tag="dmm")
                    P.op("vector", (lambda e_ch=e_ch, sh=sh, pd=pd: lambda e:
                         e.tensor_add(
                        xres[:, e_ch * ROWS + sh * 512:e_ch * ROWS + sh * 512 + 512],
                        xres[:, e_ch * ROWS + sh * 512:e_ch * ROWS + sh * 512 + 512],
                        pd[:, :]))(),
                        reads=[rd_, r_xres], writes=[r_xres], tag="dacc")
        if sup + 1 < nsup:
            wd_dma(sup + 1)

    P.op("sync", lambda e: e.dma_start(
        out=outT.rearrange("(c p) s -> p c s", p=128),
        in_=xres[:, :].rearrange("p (c s) -> p c s", s=ROWS)),
        reads=[r_xres], writes=[r_outT], dma=True, tag="out_dma")
    P.op("gpsimd", lambda e: e.nop(), reads=[r_outT, r_scr], tag="final")

    P.emit(nc)
    es.close()
    return nc


def _shard_inputs(L0_attn_x, L0_attn_Wq, L0_attn_Wk, L0_attn_Wv, L0_attn_Wo,
                  L0_attn_betas, L0_ffn_w_gate_up, L0_ffn_w_down):
    x = np.ascontiguousarray(np.asarray(L0_attn_x, np.float32)[0])
    xT = np.ascontiguousarray(x.T)
    Wq = np.asarray(L0_attn_Wq, np.float32)
    Wk = np.asarray(L0_attn_Wk, np.float32)
    Wv = np.asarray(L0_attn_Wv, np.float32)
    Wo = np.ascontiguousarray(np.asarray(L0_attn_Wo, np.float32))
    betas = np.asarray(L0_attn_betas, np.float32)[0, :, 0, :]
    wgut = np.ascontiguousarray(np.asarray(L0_ffn_w_gate_up, np.float32).T)
    wdt = np.ascontiguousarray(np.asarray(L0_ffn_w_down, np.float32).T)

    in_maps = []
    row_idx = []
    for c in range(NC):
        rows = (np.arange(NSEG)[:, None] * SEG + 64 * c
                + np.arange(64)[None, :]).reshape(-1)
        row_idx.append(rows)
        in_maps.append({
            "xT": xT,
            "xrt": np.ascontiguousarray(x[rows].T),
            "wq": np.ascontiguousarray(Wq[:, c * HPC * DK:(c + 1) * HPC * DK]),
            "wk": np.ascontiguousarray(Wk[:, c * HPC * DK:(c + 1) * HPC * DK]),
            "wv": np.ascontiguousarray(Wv[:, c * HPC * DV:(c + 1) * HPC * DV]),
            "wo": Wo,
            "wgut": wgut,
            "wdt": wdt,
            "betasT": np.ascontiguousarray(betas[c * HPC:(c + 1) * HPC].T),
        })
    return in_maps, row_idx


def kernel(**inputs):
    in_maps, row_idx = _shard_inputs(**inputs)
    nc = build_program()
    res = run_bass_kernel_spmd(nc, in_maps, core_ids=list(range(NC)))
    out = np.empty((1, S, D), np.float32)
    for c in range(NC):
        out[0, row_idx[c]] = res.results[c]["outT"].T
    return out


# version 14
# speedup vs baseline: 1.2441x; 1.1775x over previous
import contextlib
import math

import numpy as np

import concourse.bass as bass
import concourse.mybir as mybir
from concourse.bass_utils import run_bass_kernel_spmd

F32 = mybir.dt.float32
BF16 = mybir.dt.bfloat16
AF = mybir.ActivationFunctionType
ALU = mybir.AluOpType
AX = mybir.AxisListType

S, D, H, DK, DV, SEG, NSEG = 8192, 2048, 16, 128, 128, 512, 16
NC = 8
HPC = H // NC
ROWS = S // NC
NCH = D // 128
SCALE = 1.0 / math.sqrt(DK)
EPS = 1e-5
NSUP = 8
SUPI = 8192 // NSUP

EXPB = -20.0


class _Res:
    __slots__ = ("w", "r", "strict", "name")

    def __init__(self, name="", strict=False):
        self.w = {}
        self.r = {}
        self.strict = strict
        self.name = name


class _Op:
    __slots__ = ("eng", "dom", "dma", "builder", "ord", "deps", "clock",
                 "needs_inc", "val", "waits", "tag")


class _Plan:
    ENGS = ("sync", "gpsimd", "vector", "scalar", "tensor")

    def __init__(self):
        self.eng_ops = {e: [] for e in self.ENGS}
        self.dom_ord = {}
        self.dom_last = {}
        self.all = []

    def op(self, eng, builder, reads=(), writes=(), dma=False, tag="",
           after=()):
        o = _Op()
        o.eng, o.dma, o.builder, o.tag = eng, dma, builder, tag
        o.dom = ("D" + eng) if dma else eng
        o.deps = {}
        o.needs_inc = False
        o.ord = self.dom_ord.get(o.dom, 0) + 1
        self.dom_ord[o.dom] = o.ord

        def dep(p, force):
            if p is None or p is o:
                return
            if (not force) and p.dom == o.dom and (not p.dma) and (not o.dma):
                return
            cur = o.deps.get(p.dom)
            if cur is None or p.ord > cur.ord:
                o.deps[p.dom] = p

        for p in after:
            dep(p, True)
        for res in reads:
            for p in res.w.values():
                dep(p, res.strict)
        for res in writes:
            for p in res.w.values():
                dep(p, res.strict)
            for p in res.r.values():
                dep(p, res.strict or p.dma or o.dma)
        for res in reads:
            res.r[o.dom] = o
        for res in writes:
            res.w = {o.dom: o}
            res.r = {}
        self.eng_ops[eng].append(o)
        self.all.append(o)
        self.dom_last[o.dom] = o
        return o

    def fence(self):
        tails = list(self.dom_last.values())
        for eng in self.ENGS:
            self.op(eng, lambda e: e.nop(), after=tails, tag="fence")

    def resolve(self):
        last_clock = {}
        for o in self.all:
            ck = dict(last_clock.get(o.eng, ()))
            o.waits = []
            for dom in sorted(o.deps):
                p = o.deps[dom]
                if ck.get(dom, 0) >= p.ord:
                    continue
                o.waits.append(p)
                p.needs_inc = True
                for d2, v2 in p.clock.items():
                    if ck.get(d2, 0) < v2:
                        ck[d2] = v2
                if ck.get(dom, 0) < p.ord:
                    ck[dom] = p.ord
            o.clock = ck
            last_clock[o.eng] = ck
        cnt = {}
        for o in self.all:
            if o.dma:
                o.needs_inc = True
            if o.needs_inc:
                v = cnt.get(o.dom, 0) + (16 if o.dma else 1)
                cnt[o.dom] = v
                o.val = v

    def emit(self, nc):
        self.resolve()
        doms = sorted({o.dom for o in self.all if o.needs_inc}
                      | {o.dom for o in self.all for p in [None]}
                      | {p.dom for o in self.all for p in o.waits})
        with contextlib.ExitStack() as es:
            sems = {d: es.enter_context(nc.semaphore(f"sem_{d}")) for d in doms}
            block = es.enter_context(nc.Block())

            def section(engname):
                ops = self.eng_ops[engname]

                def run(e):
                    for o in ops:
                        for p in o.waits:
                            e.wait_ge(sems[p.dom], p.val)
                        ins = o.builder(e)
                        if o.tag:
                            ins.annotate(o.tag)
                        if o.needs_inc:
                            ins.then_inc(sems[o.dom], 16 if o.dma else 1)
                return run

            for engname in self.ENGS:
                if self.eng_ops[engname]:
                    getattr(block, engname)(section(engname))


def build_program(nseg=NSEG, nsup=NSUP, scr_out=False):
    nc = bass.Bass("TRN2", target_bir_lowering=False)

    xT = nc.dram_tensor("xT", [D, S], F32, kind="ExternalInput").ap()
    xrt = nc.dram_tensor("xrt", [D, ROWS], F32, kind="ExternalInput").ap()
    wq = nc.dram_tensor("wq", [D, HPC * DK], F32, kind="ExternalInput").ap()
    wk = nc.dram_tensor("wk", [D, HPC * DK], F32, kind="ExternalInput").ap()
    wv = nc.dram_tensor("wv", [D, HPC * DV], F32, kind="ExternalInput").ap()
    wo = nc.dram_tensor("wo", [D, D], F32, kind="ExternalInput").ap()
    wgut = nc.dram_tensor("wgut", [D, 2 * 8192], F32, kind="ExternalInput").ap()
    wdt = nc.dram_tensor("wdt", [8192, D], F32, kind="ExternalInput").ap()
    betasT = nc.dram_tensor("betasT", [128, HPC], F32, kind="ExternalInput").ap()
    outT = nc.dram_tensor("outT", [D, ROWS], F32, kind="ExternalOutput").ap()

    scr_kind = "ExternalOutput" if scr_out else "Internal"
    att_scr = nc.dram_tensor("att_scr", [D, ROWS], F32, kind=scr_kind).ap()

    import ml_dtypes
    tri_np = np.triu(np.ones((128, 128), np.float32)).astype(ml_dtypes.bfloat16)
    tri_dram = nc.inline_tensor(tri_np, "tri_c").ap()

    P = _Plan()
    R = lambda name, strict=False: _Res(name, strict)

    es = contextlib.ExitStack()
    esa = contextlib.ExitStack()
    sb = lambda name, shape, dt: es.enter_context(nc.sbuf_tensor(name, shape, dt))
    sba = lambda name, shape, dt: esa.enter_context(nc.sbuf_tensor(name, shape, dt))
    ps = lambda name, shape, dt=F32: es.enter_context(nc.psum_tensor(name, shape, dt))

    ident = sb("ident", [128, 128], BF16)
    ones_c = sb("ones_c", [128, 1], BF16)
    ones_r = sb("ones_r", [1, 128], BF16)
    epsc = sb("epsc", [1, 1], F32)
    identf = sba("identf", [128, 128], F32)
    ones_rf = sba("ones_rf", [1, 128], F32)
    trimask = sba("trimask", [128, 128], BF16)
    negC = sba("negC", [128, 1], F32)
    beta = sba("beta", [128, HPC], F32)
    browf = sba("browf", [1, 2 * 128], F32)
    b1rowf = sba("b1rowf", [1, 2 * 128], F32)
    wq_b = sba("wq_b", [128, NCH * HPC * DK], BF16)
    wk_b = sba("wk_b", [128, NCH * HPC * DK], BF16)
    wv_b = sba("wv_b", [128, NCH * HPC * DV], BF16)
    wo_b = sba("wo_b", [128, NCH * D], BF16)
    wsum = sba("wsum", [1, 3 * HPC * DK], BF16)
    memz_f = [sba(f"memz_f{h}", [128, DK + 1], F32) for h in range(HPC)]
    memz_b = [sba(f"memz_b{h}", [128, DK + 1], BF16) for h in range(HPC)]
    betaB = [sba(f"betaB{h}", [128, DK + 1], F32) for h in range(HPC)]
    b1mB4 = [sba(f"b1mB4{h}", [128, 512], BF16) for h in range(HPC)]
    memb_beta = [sba(f"membeta{h}", [128, DK + 1], BF16) for h in range(HPC)]

    r_ident = R("ident")
    r_identf = R("identf")
    r_epsc = R("epsc", True)
    r_negC = R("negC", True)
    r_ones_c, r_ones_r, r_ones_rf = R("ones_c"), R("ones_r"), R("ones_rf")
    r_trimask = R("trimask")
    r_beta = R("beta", True)
    r_browf, r_b1rowf = R("browf", True), R("b1rowf", True)
    r_wq, r_wk, r_wv, r_wo = R("wq_b"), R("wk_b"), R("wv_b"), R("wo_b")
    r_wsum = R("wsum", True)
    r_memz_f = [R(f"memz_f{h}") for h in range(HPC)]
    r_memz_b = [R(f"memz_b{h}") for h in range(HPC)]
    r_betaB = [R(f"betaB{h}") for h in range(HPC)]
    r_b1mB4 = [R(f"b1mB4{h}") for h in range(HPC)]
    r_membeta = [R(f"membeta{h}") for h in range(HPC)]
    r_xT = R("xT")
    r_scr = R("att_scr")
    r_outT = R("outT")

    p0 = ps("p0", [128, 512])
    p1 = ps("p1", [128, 512])
    p2 = ps("p2", [128, 512])
    p3 = ps("p3", [128, 512])
    p4 = ps("p4", [128, 512])
    p5 = ps("p5", [128, 512])
    p6 = ps("p6", [128, 512])
    p7 = ps("p7", [128, 512])
    p7b = p7[:, :].bitcast(BF16)
    psums = [p0, p1, p2, p3, p4, p5, p6, p7]
    rp = [R(f"p{i}") for i in range(8)]
    r_p7s = rp[7]
    r_p7t = [rp[7] for _ in range(HPC)]

    P.op("gpsimd", lambda e: e.memset(ident[:, :], 0.0), writes=[r_ident])
    P.op("gpsimd", lambda e: e.affine_select(
        out=ident[:, :], in_=ident[:, :], compare_op=ALU.not_equal, fill=1.0,
        base=0, pattern=[[-1, 128]], channel_multiplier=1),
        reads=[r_ident], writes=[r_ident])
    P.op("gpsimd", lambda e: e.memset(identf[:, :], 0.0), writes=[r_identf])
    P.op("gpsimd", lambda e: e.affine_select(
        out=identf[:, :], in_=identf[:, :], compare_op=ALU.not_equal, fill=1.0,
        base=0, pattern=[[-1, 128]], channel_multiplier=1),
        reads=[r_identf], writes=[r_identf])
    P.op("gpsimd", lambda e: e.memset(ones_c[:, :], 1.0), writes=[r_ones_c])
    P.op("gpsimd", lambda e: e.memset(ones_r[:, :], 1.0), writes=[r_ones_r])
    P.op("gpsimd", lambda e: e.memset(ones_rf[:, :], 1.0), writes=[r_ones_rf])
    P.op("gpsimd", lambda e: e.memset(epsc[:, :], EPS), writes=[r_epsc])
    P.op("gpsimd", lambda e: e.memset(negC[:, :], EXPB), writes=[r_negC])
    P.op("sync", lambda e: e.dma_start(out=trimask[:, :], in_=tri_dram[:, :]),
         writes=[r_trimask], dma=True)

    P.op("sync", lambda e: e.dma_start(out=beta[:, :], in_=betasT[:, :]),
         writes=[r_beta], dma=True)
    P.op("scalar", lambda e: e.activation(beta[:, :], beta[:, :], AF.Sigmoid),
         reads=[r_beta], writes=[r_beta])
    for h in range(HPC):
        P.op("tensor", (lambda h=h: lambda e: e.matmul(
            p7[0:1, 256:384], beta[:, h:h + 1], identf[:, :],
            start=True, stop=True))(),
            reads=[r_beta, r_identf], writes=[r_p7s], tag="brow_mm")
        P.op("vector", (lambda h=h: lambda e: e.tensor_copy(
            browf[0:1, h * 128:(h + 1) * 128], p7[0:1, 256:384]))(),
            reads=[r_p7s], writes=[r_browf], tag="brow_ev")
    P.op("vector", lambda e: e.tensor_scalar(b1rowf[0:1, :], browf[0:1, :],
                                             -1.0, 1.0, ALU.mult, ALU.add),
         reads=[r_browf], writes=[r_b1rowf])
    for h in range(HPC):
        P.op("tensor", (lambda h=h: lambda e: e.matmul(
            p2[:, 0:128], ones_rf[:, :], browf[0:1, h * 128:(h + 1) * 128],
            start=True, stop=True))(),
            reads=[r_ones_rf, r_browf], writes=[rp[2]], tag="betabc")
        P.op("vector", (lambda h=h: lambda e: e.tensor_copy(
            betaB[h][:, 0:128], p2[:, 0:128]))(),
            reads=[rp[2]], writes=[r_betaB[h]], tag="betabc_ev")
        P.op("vector", (lambda h=h: lambda e: e.memset(
            betaB[h][:, 128:129], 1.0))(),
            reads=[r_betaB[h]], writes=[r_betaB[h]])
        P.op("tensor", (lambda h=h: lambda e: e.matmul(
            p3[:, 0:128], ones_rf[:, :], b1rowf[0:1, h * 128:(h + 1) * 128],
            start=True, stop=True))(),
            reads=[r_ones_rf, r_b1rowf], writes=[rp[3]], tag="b1mbc")
        for t in range(4):
            P.op("vector", (lambda h=h, t=t: lambda e: e.tensor_copy(
                b1mB4[h][:, t * 128:(t + 1) * 128], p3[:, 0:128]))(),
                reads=[rp[3]], writes=[r_b1mB4[h]], tag="b1mbc_ev")

    def cast_load(dst, dstv, src, res, n3):
        P.op("gpsimd", lambda e: e.dma_start(out=dstv, in_=src), writes=[res],
             dma=True, tag="wload")

    wq_v = wq_b[:, :].rearrange("p (c m) -> p c m", c=NCH)
    wk_v = wk_b[:, :].rearrange("p (c m) -> p c m", c=NCH)
    wv_v = wv_b[:, :].rearrange("p (c m) -> p c m", c=NCH)
    cast_load(wq_b, wq_v, wq.rearrange("(c p) m -> p c m", p=128), r_wq, NCH)
    cast_load(wk_b, wk_v, wk.rearrange("(c p) m -> p c m", p=128), r_wk, NCH)
    cast_load(wv_b, wv_v, wv.rearrange("(c p) m -> p c m", p=128), r_wv, NCH)
    wo_v = wo_b[:, :].rearrange("p (c m) -> p c m", c=NCH)

    def wo_load():
        P.op("gpsimd", lambda e: e.dma_start(
            out=wo_v, in_=wo.rearrange("(c p) m -> p c m", p=128)),
            writes=[r_wo], dma=True, tag="wo_load")

    for wi, (wbuf, res) in enumerate(((wq_b, r_wq), (wk_b, r_wk), (wv_b, r_wv))):
        W = HPC * DK
        for c in range(NCH):
            P.op("tensor", (lambda wbuf=wbuf, c=c, W=W: lambda e: e.matmul(
                p0[0:1, 0:W], ones_c[:, :], wbuf[:, c * W:(c + 1) * W],
                start=(c == 0), stop=(c == NCH - 1)))(),
                reads=[res, r_ones_c], writes=[rp[0]], tag="wsum_mm")
        P.op("scalar", (lambda wi=wi, W=W: lambda e: e.mul(
            wsum[0:1, wi * W:(wi + 1) * W], p0[0:1, 0:W], -1.0))(),
            reads=[rp[0]], writes=[r_wsum], tag="wsum_evac")

    for h in range(HPC):
        P.op("vector", (lambda h=h: lambda e: e.memset(memz_f[h][:, 0:DK], 0.0))(),
             writes=[r_memz_f[h]])
        P.op("vector", (lambda h=h: lambda e: e.memset(
            memz_f[h][:, DK:DK + 1], 1.0 / DK))(),
            reads=[r_memz_f[h]], writes=[r_memz_f[h]])
        P.op("scalar", (lambda h=h: lambda e: e.copy(
            memz_b[h][:, :], memz_f[h][:, :]))(),
            reads=[r_memz_f[h]], writes=[r_memz_b[h]])
        P.op("vector", (lambda h=h: lambda e: e.tensor_mul(
            memb_beta[h][:, :], memz_f[h][:, :], betaB[h][:, :]))(),
            reads=[r_memz_f[h], r_betaB[h]], writes=[r_membeta[h]],
            tag="membeta0")

    xbf = [sba(f"xbf{i}", [128, NCH * SEG], BF16) for i in range(2)]
    r_xbf = [[R(f"xbf{i}c{c}") for c in range(NCH)] for i in range(2)]
    x2b = [sba(f"x2b{i}", [128, 512], BF16) for i in range(2)]
    r_x2b = [R(f"x2b{i}") for i in range(2)]
    rowf = sba("rowf", [1, 1024], F32)
    r_rowf = R("rowf", True)
    rows_b = [sba(f"rows_b{i}", [1, 1024], BF16) for i in range(2)]
    r_rows_b = [R(f"rows_b{i}", True) for i in range(2)]
    rbc = [sba(f"rbc{i}", [128, 512], F32) for i in range(2)]
    r_rbc = [R(f"rbc{i}") for i in range(2)]

    def _bufs2(nm, w=512, dt=BF16):
        return ([[sba(f"{nm}{p}{h}", [128, w], dt) for h in range(HPC)]
                 for p in range(2)],
                [[R(f"{nm}{p}{h}") for h in range(HPC)] for p in range(2)])

    qS, r_qS = _bufs2("qS")
    kT, r_kT = _bufs2("kT")
    sq, r_sq = _bufs2("sq")
    skT, r_skT = _bufs2("skT")
    skt, r_skt = _bufs2("skt")
    vtk, r_vtk = _bufs2("vtk")
    vts, r_vts = _bufs2("vts")

    def _bufs1(nm, w=512, dt=BF16, strict=False):
        return ([sba(f"{nm}{h}", [128, w], dt) for h in range(HPC)],
                [R(f"{nm}{h}", strict) for h in range(HPC)])

    vdv, r_vdv = _bufs1("vdv")
    dlt, r_dlt = _bufs1("dlt", 4 * (DV + 1))
    sg1 = sba("sg1", [128, 512], BF16)
    sg2 = sba("sg2", [128, 512], BF16)
    r_sg1, r_sg2 = R("sg1"), R("sg2")
    Pb = [sba(f"Pb{h}", [128, 4 * 512], BF16) for h in range(HPC)]
    r_Pb = [[R(f"Pb{h}c{tc}") for tc in range(4)] for h in range(HPC)]
    attq, r_attq = _bufs1("attq")
    attt, r_attt = _bufs1("attt")
    tmpc, r_tmpc = _bufs1("tmpc", 128, F32)
    zrot = sba("zrot", [128, 128], F32)
    r_zrot = R("zrot")
    P.op("vector", lambda e: e.memset(zrot[:, :], 0.0), writes=[r_zrot])
    rds, r_rds = _bufs1("rds", 8, F32, True)
    rnd, r_rnd = _bufs1("rnd", 4, F32, True)
    lnrow = sba("lnrow", [1, 512], F32)
    r_lnrow = R("lnrow", True)
    attP = [sba(f"attP{i}", [128, 4 * 512], BF16) for i in range(2)]
    r_attP = [R(f"attP{i}") for i in range(2)]
    outb = sba("outb", [128, NCH * 64], F32)
    r_outb = R("outb")

    for h in range(HPC):
        P.op("vector", (lambda h=h: lambda e: e.memset(
            dlt[h][:, :], -1.0))(), writes=[r_dlt[h]])

    def seg_dma(g):
        b = g % 2
        src = xT[:, g * SEG:(g + 1) * SEG].rearrange("(c p) s -> p c s", p=128)
        dst = xbf[b][:, :].rearrange("p (c s) -> p c s", s=SEG)
        P.op("gpsimd", (lambda src=src, dst=dst: lambda e: e.dma_start(
            out=dst, in_=src))(), reads=[r_xT], writes=r_xbf[b], dma=True,
            tag=f"xdma{g}")

    def stats_mm(g):
        b = g % 2
        for c in range(NCH):
            P.op("vector", (lambda b=b, c=c: lambda e: e.tensor_mul(
                x2b[c % 2][:, :], xbf[b][:, c * SEG:c * SEG + 512],
                xbf[b][:, c * SEG:c * SEG + 512]))(),
                reads=[r_xbf[b][c]], writes=[r_x2b[c % 2]], tag="x2")
            P.op("tensor", (lambda b=b, c=c: lambda e: e.matmul(
                p0[64:65, 0:512], ones_c[:, :], xbf[b][:, c * SEG:c * SEG + 512],
                start=(c == 0), stop=(c == NCH - 1)))(),
                reads=[r_xbf[b][c], r_ones_c], writes=[rp[0]], tag="mmM")
            P.op("tensor", (lambda c=c: lambda e: e.matmul(
                p1[64:65, 0:512], ones_c[:, :], x2b[c % 2][:, :],
                start=(c == 0), stop=(c == NCH - 1)))(),
                reads=[r_x2b[c % 2], r_ones_c], writes=[rp[1]], tag="mmS")

    def stats_evac(g):
        b = g % 2
        rf, rb = rowf, rows_b[b]
        rrf, rrb = r_rowf, r_rows_b[b]
        P.op("vector", (lambda rf=rf: lambda e: e.tensor_scalar_mul(
            rf[0:1, 0:512], p0[64:65, 0:512], 1.0 / D))(),
            reads=[rp[0]], writes=[rrf], tag="rowm")
        P.op("vector", (lambda rf=rf: lambda e: e.tensor_scalar_mul(
            rf[0:1, 512:1024], p1[64:65, 0:512], 1.0 / D))(),
            reads=[rp[1]], writes=[rrf], tag="rowsq")
        P.op("scalar", (lambda rf=rf, rb=rb: lambda e: e.copy(
            rb[0:1, 512:1024], rf[0:1, 0:512]))(),
            reads=[rrf], writes=[rrb], tag="mrow")
        P.op("scalar", (lambda rf=rf: lambda e: e.activation(
            lnrow[0:1, :], rf[0:1, 0:512], AF.Square))(),
            reads=[rrf], writes=[r_lnrow], tag="msq")
        P.op("vector", (lambda rf=rf: lambda e: e.tensor_sub(
            rf[0:1, 512:1024], rf[0:1, 512:1024], lnrow[0:1, :]))(),
            reads=[rrf, r_lnrow], writes=[rrf], tag="var")
        P.op("scalar", (lambda rf=rf: lambda e: e.activation(
            rf[0:1, 512:1024], rf[0:1, 512:1024], AF.Ln,
            bias=epsc[0:1, 0:1]))(),
            reads=[rrf, r_epsc], writes=[rrf], tag="ln")
        P.op("scalar", (lambda rf=rf, rb=rb: lambda e: e.activation(
            rb[0:1, 0:512], rf[0:1, 512:1024], AF.Exp, scale=-0.5))(),
            reads=[rrf], writes=[rrb], tag="exp")

    def stats_rows(g):
        b = g % 2
        P.op("tensor", (lambda b=b: lambda e: e.matmul(
            p6[:, :], ones_r[:, :], rows_b[b][0:1, 0:512],
            start=True, stop=True))(),
            reads=[r_rows_b[b], r_ones_r], writes=[rp[6]], tag="rbc_mm")
        P.op("vector", (lambda b=b: lambda e: e.tensor_copy(
            rbc[b][:, :], p6[:, :]))(),
            reads=[rp[6]], writes=[r_rbc[b]], tag="rbc_evac")

    def proj(g, h, wbuf, r_wbuf, wslot, wsoff, dst, r_dst, pbank, rb_,
             scl=None):
        b = g % 2
        W = HPC * DK
        for c in range(NCH):
            P.op("tensor", (lambda c=c, wbuf=wbuf, wslot=wslot, b=b,
                            pbank=pbank: lambda e:
                 e.matmul(pbank[:, :], wbuf[:, c * W + wslot:c * W + wslot + 128],
                          xbf[b][:, c * SEG:c * SEG + 512],
                          start=(c == 0), stop=False))(),
                reads=[r_wbuf, r_xbf[b][c]], writes=[rb_], tag="prj")
        P.op("tensor", (lambda wsoff=wsoff, wslot=wslot, b=b,
                        pbank=pbank: lambda e: e.matmul(
            pbank[:, :], wsum[0:1, wsoff + wslot:wsoff + wslot + 128],
            rows_b[b][0:1, 512:1024], start=False, stop=True))(),
            reads=[r_wsum, r_rows_b[b]], writes=[rb_], tag="rank1")
        if scl is None:
            P.op("vector", (lambda dst=dst, b=b, pbank=pbank: lambda e:
                 e.tensor_mul(dst[:, :], pbank[:, :], rbc[b][:, :]))(),
                reads=[rb_, r_rbc[b]], writes=[r_dst], tag="corr1")
        else:
            P.op("vector", (lambda dst=dst, b=b, scl=scl, pbank=pbank: lambda e:
                 e.scalar_tensor_tensor(dst[:, :], pbank[:, :], scl, rbc[b][:, :],
                                        ALU.mult, ALU.mult))(),
                reads=[rb_, r_rbc[b]], writes=[r_dst], tag="corr1")

    def sigma(src, r_src, dst, r_dst, comp=1.0):
        P.op("scalar", (lambda src=src: lambda e: e.activation(
            sg1[:, :], src[:, :], AF.Relu, scale=-1.0))(),
            reads=[r_src], writes=[r_sg1], tag="sgrelu1")
        P.op("scalar", (lambda comp=comp: lambda e: e.activation(
            sg2[:, :], sg1[:, :], AF.Exp, scale=-comp))(),
            reads=[r_sg1], writes=[r_sg2], tag="sgexp")
        P.op("scalar", (lambda src=src: lambda e: e.activation(
            sg1[:, :], src[:, :], AF.Relu))(),
            reads=[r_src], writes=[r_sg1], tag="sgrelu2")
        if comp == 1.0:
            P.op("vector", (lambda dst=dst: lambda e: e.tensor_add(
                dst[:, :], sg1[:, :], sg2[:, :]))(),
                reads=[r_sg1, r_sg2], writes=[r_dst], tag="sgadd")
        else:
            P.op("vector", (lambda dst=dst, comp=comp: lambda e:
                 e.scalar_tensor_tensor(dst[:, :], sg1[:, :], comp, sg2[:, :],
                                        ALU.mult, ALU.add))(),
                reads=[r_sg1, r_sg2], writes=[r_dst], tag="sgadd")

    def transp4(src, r_src, dst, r_dst):
        for t in range(4):
            P.op("tensor", (lambda t=t, src=src: lambda e: e.transpose(
                p7b[:, t * 128:(t + 1) * 128], src[:, t * 128:(t + 1) * 128],
                ident[:, :]))(),
                reads=[r_src, r_ident], writes=[r_p7s], tag="tr4")
        P.op("vector", (lambda dst=dst: lambda e: e.tensor_copy(
            dst[:, :], p7b[:, 0:512]))(),
            reads=[r_p7s], writes=[r_dst], tag="tr4_evac")

    def qkvA(g, h):
        b = g % 2
        ws = h * DK
        W = HPC * DK
        bk = [(p2, rp[2]), (p3, rp[3])]
        proj(g, h, wq_b, r_wq, ws, 0, qS[b][h], r_qS[b][h],
             *bk[(h * 3 + 0) % 2], scl=SCALE)
        proj(g, h, wk_b, r_wk, ws, W, kT[b][h], r_kT[b][h],
             *bk[(h * 3 + 1) % 2])
        proj(g, h, wv_b, r_wv, ws, 2 * W, vdv[h], r_vdv[h],
             *bk[(h * 3 + 2) % 2])

    def qkvSig(g, h):
        b = g % 2
        sigma(qS[b][h], r_qS[b][h], sq[b][h], r_sq[b][h], comp=math.sqrt(DK))
        sigma(kT[b][h], r_kT[b][h], skT[b][h], r_skT[b][h])

    def qkvTr(g, h):
        b = g % 2
        transp4(vdv[h], r_vdv[h], vtk[b][h], r_vtk[b][h])
        P.op("vector", (lambda b=b, h=h: lambda e: e.tensor_mul(
            vts[b][h][:, :], vtk[b][h][:, :], b1mB4[h][:, :]))(),
            reads=[r_vtk[b][h], r_b1mB4[h]], writes=[r_vts[b][h]], tag="vts")
        transp4(skT[b][h], r_skT[b][h], skt[b][h], r_skt[b][h])

    def sftT(g, h):
        b = g % 2
        for tc in range(4):
            n = 512 - tc * 128
            pbank, rb_ = (p0, rp[0]) if tc % 2 == 0 else (p1, rp[1])
            P.op("tensor", (lambda tc=tc, h=h, b=b, n=n, pbank=pbank: lambda e:
                 e.matmul(pbank[:, 0:n], kT[b][h][:, tc * 128:(tc + 1) * 128],
                          qS[b][h][:, tc * 128:512], start=True, stop=True))(),
                reads=[r_kT[b][h], r_qS[b][h]], writes=[rb_], tag="scoreT")
            P.op("scalar", (lambda tc=tc, h=h, n=n, pbank=pbank: lambda e:
                 e.activation(Pb[h][:, tc * 512 + tc * 128:tc * 512 + 512],
                              pbank[:, 0:n], AF.Exp, bias=negC[:, 0:1]))(),
                reads=[rb_, r_negC], writes=[r_Pb[h][tc]], tag="expT")
            P.op("vector", (lambda tc=tc, h=h: lambda e: e.tensor_mul(
                Pb[h][:, tc * 512 + tc * 128:tc * 512 + tc * 128 + 128],
                Pb[h][:, tc * 512 + tc * 128:tc * 512 + tc * 128 + 128],
                trimask[:, :]))(),
                reads=[r_Pb[h][tc], r_trimask], writes=[r_Pb[h][tc]],
                tag="trimul")

    def memA(g, h):
        b = g % 2
        pM, rM = (p4, rp[4]) if h == 0 else (p5, rp[5])
        base = 384 + h * 16
        for sc in range(4):
            P.op("tensor", (lambda sc=sc, h=h, b=b, pM=pM: lambda e: e.matmul(
                pM[:, sc * 128:(sc + 1) * 128],
                sq[b][h][:, sc * 128:(sc + 1) * 128],
                memb_beta[h][:, 0:DK], start=True, stop=True))(),
                reads=[r_sq[b][h], r_membeta[h]], writes=[rM], tag="attM")
        for sc in range(4):
            P.op("tensor", (lambda sc=sc, h=h, b=b, base=base: lambda e:
                 e.matmul(p7[:, base + sc:base + sc + 1],
                          sq[b][h][:, sc * 128:(sc + 1) * 128],
                          memb_beta[h][:, DK:DK + 1], start=True, stop=True))(),
                reads=[r_sq[b][h], r_membeta[h]], writes=[r_p7t[h]], tag="attD")

    def dotB(g, h):
        b = g % 2
        base = 384 + h * 16
        for sc in range(4):
            for tc in range(sc + 1):
                P.op("tensor", (lambda sc=sc, tc=tc, h=h, b=b: lambda e:
                     e.matmul(p6[:, sc * 128:(sc + 1) * 128],
                              Pb[h][:, tc * 512 + sc * 128:
                                   tc * 512 + (sc + 1) * 128],
                              vts[b][h][:, tc * 128:(tc + 1) * 128],
                              start=(tc == 0), stop=(tc == sc)))(),
                    reads=[r_Pb[h][tc], r_vts[b][h]], writes=[rp[6]],
                    tag="attdot")
                P.op("tensor", (lambda sc=sc, tc=tc, h=h, base=base: lambda e:
                     e.matmul(p7[:, base + 4 + sc:base + 5 + sc],
                              Pb[h][:, tc * 512 + sc * 128:
                                   tc * 512 + (sc + 1) * 128],
                              ones_c[:, :], start=(tc == 0), stop=(tc == sc)))(),
                    reads=[r_Pb[h][tc], r_ones_c], writes=[r_p7t[h]],
                    tag="sume")
        P.op("vector", (lambda h=h, base=base: lambda e: e.reciprocal(
            rds[h][:, 0:8], p7[:, base:base + 8]))(),
            reads=[r_p7t[h]], writes=[r_rds[h]], tag="rcp_ds")

    def cmbT(g, h):
        pM, rM = (p4, rp[4]) if h == 0 else (p5, rp[5])
        for sc in range(4):
            P.op("vector", (lambda sc=sc, h=h: lambda e:
                 e.scalar_tensor_tensor(
                tmpc[h][:, :], p6[:, sc * 128:(sc + 1) * 128],
                rds[h][:, 4 + sc:5 + sc], zrot[:, :], ALU.mult, ALU.add))(),
                reads=[rp[6], r_rds[h], r_zrot], writes=[r_tmpc[h]], tag="cmbA")
            P.op("vector", (lambda sc=sc, h=h, pM=pM: lambda e:
                 e.scalar_tensor_tensor(
                attq[h][:, sc * 128:(sc + 1) * 128],
                pM[:, sc * 128:(sc + 1) * 128], rds[h][:, sc:sc + 1],
                tmpc[h][:, :], ALU.mult, ALU.add))(),
                reads=[rM, r_rds[h], r_tmpc[h]], writes=[r_attq[h]], tag="cmbB")
        for t in range(4):
            P.op("tensor", (lambda t=t, h=h: lambda e: e.transpose(
                p7b[:, t * 128:(t + 1) * 128], attq[h][:, t * 128:(t + 1) * 128],
                ident[:, :]))(),
                reads=[r_attq[h], r_ident], writes=[r_p7s], tag="attT")
        P.op("vector", (lambda h=h: lambda e: e.tensor_copy(
            attt[h][:, :], p7b[:, 0:512]))(),
            reads=[r_p7s], writes=[r_attt[h]], tag="attT_ev")
        pp = (g // 2) % 2
        blk = (g % 2) * 2 + h
        dstv = attP[pp][:, blk * 512:(blk + 1) * 512].rearrange(
            "p (c a) -> p a c", a=32)
        srcv = attt[h][:, :].rearrange("p (a c) -> p a c", c=16)
        P.op("scalar", (lambda dstv=dstv, srcv=srcv: lambda e: e.copy(
            dstv, srcv))(),
            reads=[r_attt[h]], writes=[r_attP[pp]], tag="attT_pk")

    def deltaC(g, h):
        b = g % 2
        pM, rM = (p4, rp[4]) if h == 0 else (p5, rp[5])
        base = 384 + h * 16
        for sc in range(4):
            P.op("tensor", (lambda sc=sc, h=h, b=b, pM=pM: lambda e: e.matmul(
                pM[:, sc * 128:(sc + 1) * 128],
                skT[b][h][:, sc * 128:(sc + 1) * 128], memz_b[h][:, 0:DK],
                start=True, stop=True))(),
                reads=[r_skT[b][h], r_memz_b[h]], writes=[rM], tag="nk")
        for sc in range(4):
            P.op("tensor", (lambda sc=sc, h=h, b=b, base=base: lambda e:
                 e.matmul(p7[:, base + 8 + sc:base + 9 + sc],
                          skT[b][h][:, sc * 128:(sc + 1) * 128],
                          memz_b[h][:, DK:DK + 1], start=True, stop=True))(),
                reads=[r_skT[b][h], r_memz_b[h]], writes=[r_p7t[h]], tag="nd")
        P.op("vector", (lambda h=h, base=base: lambda e: e.reciprocal(
            rnd[h][:, :], p7[:, base + 8:base + 12]))(),
            reads=[r_p7t[h]], writes=[r_rnd[h]], tag="rcp_nd")
        for sc in range(4):
            P.op("vector", (lambda sc=sc, h=h, b=b, pM=pM: lambda e:
                 e.scalar_tensor_tensor(
                dlt[h][:, sc * 129:sc * 129 + 128],
                pM[:, sc * 128:(sc + 1) * 128], rnd[h][:, sc:sc + 1],
                vtk[b][h][:, sc * 128:(sc + 1) * 128],
                ALU.mult, ALU.subtract))(),
                reads=[rM, r_rnd[h], r_vtk[b][h]], writes=[r_dlt[h]],
                tag="delta")
        for sc in range(4):
            P.op("tensor", (lambda sc=sc, h=h, b=b, pM=pM: lambda e: e.matmul(
                pM[:, 0:129], skt[b][h][:, sc * 128:(sc + 1) * 128],
                dlt[h][:, sc * 129:sc * 129 + 129],
                start=(sc == 0), stop=(sc == 3)))(),
                reads=[r_skt[b][h], r_dlt[h]], writes=[rM], tag="upd")
        P.op("vector", (lambda h=h, pM=pM: lambda e: e.tensor_sub(
            memz_f[h][:, :], memz_f[h][:, :], pM[:, 0:129]))(),
            reads=[rM, r_memz_f[h]], writes=[r_memz_f[h]], tag="memupd")
        P.op("scalar", (lambda h=h: lambda e: e.copy(
            memz_b[h][:, :], memz_f[h][:, :]))(),
            reads=[r_memz_f[h]], writes=[r_memz_b[h]], tag="memcast")
        P.op("vector", (lambda h=h: lambda e: e.tensor_mul(
            memb_beta[h][:, :], memz_f[h][:, :], betaB[h][:, :]))(),
            reads=[r_memz_f[h], r_betaB[h]], writes=[r_membeta[h]],
            tag="membeta")

    def wo_pair(p_idx):
        pp = p_idx % 2
        attv = attP[pp][:, :].rearrange("p (b s) -> p b s", s=512)
        for half in range(2):
            for eg in range(2):
                pbank, rb_ = (p2, rp[2]) if eg % 2 == 0 else (p3, rp[3])
                for ei in range(4):
                    e_ch = half * 8 + eg * 4 + ei
                    for j in range(NCH):
                        P.op("tensor", (lambda e_ch=e_ch, ei=ei, j=j,
                                        pbank=pbank, attv=attv: lambda e:
                             e.matmul(
                            pbank[:, ei * 128:(ei + 1) * 128],
                            wo_b[:, j * D + e_ch * 128:j * D + (e_ch + 1) * 128],
                            attv[:, :, j * 32:(j + 1) * 32],
                            start=(j == 0), stop=(j == NCH - 1)))(),
                            reads=[r_wo, r_attP[pp]], writes=[rb_], tag="womm")
                P.op("vector", (lambda eg=eg, pbank=pbank: lambda e:
                     e.tensor_copy(
                    outb[:, eg * 512:(eg + 1) * 512], pbank[:, :]))(),
                    reads=[rb_], writes=[r_outb], tag="woev")
            dst = att_scr[half * 1024:(half + 1) * 1024,
                          p_idx * 128:(p_idx + 1) * 128].rearrange(
                "(c p) r -> p c r", p=128)
            srcv = outb[:, :].rearrange("p (c r) -> p c r", r=128)
            P.op("sync", (lambda dst=dst, srcv=srcv: lambda e: e.dma_start(
                out=dst, in_=srcv))(), reads=[r_outb], writes=[r_scr],
                dma=True, tag="scrdma")

    if nseg > 0:
        seg_dma(0)
        stats_mm(0)
        stats_evac(0)
        stats_rows(0)
        qkvA(0, 0)
        qkvA(0, 1)
        seg_dma(1)
        wo_load()
        qkvSig(0, 0)
        qkvSig(0, 1)
        sftT(0, 0)
        sftT(0, 1)
        qkvTr(0, 0)
        qkvTr(0, 1)
    for g in range(nseg):
        nxt = g + 1 < nseg
        if nxt:
            stats_mm(g + 1)
        memA(g, 0)
        dotB(g, 0)
        if nxt:
            stats_evac(g + 1)
        cmbT(g, 0)
        memA(g, 1)
        dotB(g, 1)
        cmbT(g, 1)
        deltaC(g, 0)
        if nxt:
            stats_rows(g + 1)
            qkvA(g + 1, 0)
            qkvSig(g + 1, 0)
            sftT(g + 1, 0)
        deltaC(g, 1)
        if nxt:
            qkvA(g + 1, 1)
            qkvSig(g + 1, 1)
            sftT(g + 1, 1)
            qkvTr(g + 1, 0)
            qkvTr(g + 1, 1)
        if g % 2 == 1:
            wo_pair(g // 2)
        if g + 2 < nseg:
            seg_dma(g + 2)

    P.fence()
    esa.close()
    xres = sb("xres", [128, NCH * ROWS], F32)
    xln = sb("xln", [128, NCH * ROWS], BF16)
    hbf = sb("hbf", [128, 8 * ROWS], BF16)
    QSUP = SUPI // 4
    wgg = [sb(f"wgg{i}", [128, NCH * QSUP], BF16) for i in range(2)]
    wgu = [sb(f"wgu{i}", [128, NCH * QSUP], BF16) for i in range(2)]
    wdb = sb("wdb", [128, 8 * D], BF16)
    sgf = sb("sgf", [128, 2048], F32)
    x2f = sb("x2f", [128, 1024], BF16)
    rowF = sb("rowF", [1, 2048], F32)
    rowFb = sb("rowFb", [1, 2048], BF16)
    rbc2 = sb("rbc2", [128, 1024], F32)
    mbc2 = sb("mbc2", [128, 1024], F32)
    r_xres, r_xln, r_hbf = R("xres"), R("xln"), R("hbf")
    r_wgg = [R(f"wgg{i}") for i in range(2)]
    r_wgu = [R(f"wgu{i}") for i in range(2)]
    r_wdb = R("wdb")
    r_sgf, r_x2f = R("sgf"), R("x2f")
    r_rowF, r_rowFb = R("rowF", True), R("rowFb", True)
    r_rbc2, r_mbc2 = R("rbc2"), R("mbc2")

    r_xresg = [R(f"xresg{i}") for i in range(4)]
    for grp in range(4):
        cs = slice(grp * 4 * ROWS, (grp + 1) * 4 * ROWS)
        P.op("sync", (lambda grp=grp, cs=cs: lambda e: e.dma_start(
            out=xres[:, cs].rearrange("p (c s) -> p c s", s=ROWS),
            in_=xrt[grp * 512:(grp + 1) * 512, :].rearrange(
                "(c p) s -> p c s", p=128)))(),
            writes=[r_xresg[grp]], dma=True, tag="xrt_dma")
        P.op("gpsimd", (lambda grp=grp, cs=cs: lambda e: e.dma_start(
            out=xres[:, cs].rearrange("p (c s) -> p c s", s=ROWS),
            in_=att_scr[grp * 512:(grp + 1) * 512, :].rearrange(
                "(c p) s -> p c s", p=128), accum_op=ALU.add))(),
            reads=[r_scr, r_xresg[grp]], writes=[r_xresg[grp]], dma=True,
            tag="scr_acc")
        for c in range(grp * 4, grp * 4 + 4):
            P.op("scalar", (lambda c=c, grp=grp: lambda e: e.copy(
                xln[:, c * ROWS:(c + 1) * ROWS],
                xres[:, c * ROWS:(c + 1) * ROWS]))(),
                reads=[r_xresg[grp]], writes=[r_xln], tag="xcast")
    P.op("vector", lambda e: e.nop(),
         reads=[r_xresg[0], r_xresg[1], r_xresg[2], r_xresg[3]],
         writes=[r_xres], tag="xres_join")
    for c in range(NCH):
        P.op("scalar", (lambda c=c: lambda e: e.activation(
            x2f[:, :], xln[:, c * ROWS:(c + 1) * ROWS], AF.Square))(),
            reads=[r_xln], writes=[r_x2f], tag="fx2")
        for sh in range(2):
            P.op("tensor", (lambda c=c, sh=sh: lambda e: e.matmul(
                p0[0:1, 0:512] if sh == 0 else p2[0:1, 0:512], ones_c[:, :],
                xln[:, c * ROWS + sh * 512:c * ROWS + sh * 512 + 512],
                start=(c == 0), stop=(c == NCH - 1)))(),
                reads=[r_xln, r_ones_c], writes=[rp[0] if sh == 0 else rp[2]],
                tag="fmmM")
            P.op("tensor", (lambda c=c, sh=sh: lambda e: e.matmul(
                p1[0:1, 0:512] if sh == 0 else p3[0:1, 0:512], ones_c[:, :],
                x2f[:, sh * 512:sh * 512 + 512],
                start=(c == 0), stop=(c == NCH - 1)))(),
                reads=[r_x2f, r_ones_c], writes=[rp[1] if sh == 0 else rp[3]],
                tag="fmmS")
    for sh, (pm, pv, rm, rv) in enumerate(((p0, p1, rp[0], rp[1]),
                                           (p2, p3, rp[2], rp[3]))):
        o = sh * 512
        P.op("vector", (lambda o=o, pm=pm: lambda e: e.tensor_scalar_mul(
            rowF[0:1, o:o + 512], pm[0:1, 0:512], 1.0 / D))(),
            reads=[rm], writes=[r_rowF], tag="frowm")
        P.op("vector", (lambda o=o: lambda e: e.tensor_mul(
            rowF[0:1, 1024 + o:1024 + o + 512], rowF[0:1, o:o + 512],
            rowF[0:1, o:o + 512]))(),
            reads=[r_rowF], writes=[r_rowF], tag="fmsq")
        P.op("vector", (lambda o=o, pv=pv: lambda e: e.scalar_tensor_tensor(
            rowF[0:1, 1024 + o:1024 + o + 512], pv[0:1, 0:512], 1.0 / D,
            rowF[0:1, 1024 + o:1024 + o + 512], ALU.mult, ALU.subtract))(),
            reads=[rv, r_rowF], writes=[r_rowF], tag="fvar")
    P.op("scalar", lambda e: e.activation(rowF[0:1, 1024:2048],
                                          rowF[0:1, 1024:2048], AF.Ln,
                                          bias=epsc[0:1, 0:1]),
         reads=[r_rowF, r_epsc], writes=[r_rowF], tag="fln")
    P.op("scalar", lambda e: e.activation(rowF[0:1, 1024:2048],
                                          rowF[0:1, 1024:2048], AF.Exp,
                                          scale=-0.5),
         reads=[r_rowF], writes=[r_rowF], tag="fexp")
    P.op("vector", lambda e: e.tensor_mul(rowF[0:1, 0:1024], rowF[0:1, 0:1024],
                                          rowF[0:1, 1024:2048]),
         reads=[r_rowF], writes=[r_rowF], tag="fmr")
    P.op("scalar", lambda e: e.copy(rowFb[0:1, :], rowF[0:1, :]),
         reads=[r_rowF], writes=[r_rowFb], tag="frowb")
    for sh in range(2):
        P.op("tensor", (lambda sh=sh: lambda e: e.matmul(
            p5[:, 0:512], ones_r[:, :],
            rowFb[0:1, 1024 + sh * 512:1024 + sh * 512 + 512],
            start=True, stop=True))(),
            reads=[r_rowFb, r_ones_r], writes=[rp[5]], tag="frbc")
        P.op("vector", (lambda sh=sh: lambda e: e.tensor_copy(
            rbc2[:, sh * 512:sh * 512 + 512], p5[:, 0:512]))(),
            reads=[rp[5]], writes=[r_rbc2], tag="frbc_ev")
        P.op("tensor", (lambda sh=sh: lambda e: e.matmul(
            p6[:, 0:512], ones_r[:, :],
            rowFb[0:1, sh * 512:sh * 512 + 512], start=True, stop=True))(),
            reads=[r_rowFb, r_ones_r], writes=[rp[6]], tag="fmbc")
        P.op("vector", (lambda sh=sh: lambda e: e.tensor_copy(
            mbc2[:, sh * 512:sh * 512 + 512], p6[:, 0:512]))(),
            reads=[rp[6]], writes=[r_mbc2], tag="fmbc_ev")
    for c in range(NCH):
        P.op("vector", (lambda c=c: lambda e: e.tensor_mul(
            sgf[:, 0:1024], xres[:, c * ROWS:(c + 1) * ROWS], rbc2[:, :]))(),
            reads=[r_xres, r_rbc2], writes=[r_sgf], tag="fnorm1")
        P.op("vector", (lambda c=c: lambda e: e.tensor_sub(
            xln[:, c * ROWS:(c + 1) * ROWS], sgf[:, 0:1024], mbc2[:, :]))(),
            reads=[r_sgf, r_mbc2], writes=[r_xln], tag="fnorm2")

    def gate_dma(sup, quarter):
        sl = quarter % 2
        o0 = sup * SUPI + quarter * QSUP
        gsrc = wgut[:, o0:o0 + QSUP].rearrange("(c p) o -> p c o", p=128)
        usrc = wgut[:, 8192 + o0:8192 + o0 + QSUP].rearrange(
            "(c p) o -> p c o", p=128)
        P.op("gpsimd", (lambda gsrc=gsrc, sl=sl: lambda e: e.dma_start(
            out=wgg[sl][:, :].rearrange("p (c o) -> p c o", o=QSUP),
            in_=gsrc))(), writes=[r_wgg[sl]], dma=True, tag="wgg_dma")
        P.op("gpsimd", (lambda usrc=usrc, sl=sl: lambda e: e.dma_start(
            out=wgu[sl][:, :].rearrange("p (c o) -> p c o", o=QSUP),
            in_=usrc))(), writes=[r_wgu[sl]], dma=True, tag="wgu_dma")

    def wd_dma(sup):
        wsrc = wdt[sup * SUPI:(sup + 1) * SUPI, :].rearrange(
            "(b p) h2 -> p b h2", p=128)
        P.op("gpsimd", (lambda wsrc=wsrc: lambda e: e.dma_start(
            out=wdb[:, :].rearrange("p (b h2) -> p b h2", h2=D), in_=wsrc))(),
            writes=[r_wdb], dma=True, tag="wd_dma")

    gate_dma(0, 0)
    gate_dma(0, 1)
    wd_dma(0)
    for sup in range(nsup):
        for quarter in range(4):
            sl = quarter % 2
            for jj in range(2):
                j = quarter * 2 + jj
                for sh in range(2):
                    k4 = (j % 2) * 4
                    pg, rg = psums[k4 + 2 * sh], rp[k4 + 2 * sh]
                    pu, ru = psums[k4 + 2 * sh + 1], rp[k4 + 2 * sh + 1]
                    for c in range(NCH):
                        P.op("tensor", (lambda c=c, jj=jj, sh=sh, pg=pg, sl=sl:
                             lambda e: e.matmul(
                            pg[:, :],
                            wgg[sl][:, c * QSUP + jj * 128:c * QSUP + (jj + 1) * 128],
                            xln[:, c * ROWS + sh * 512:c * ROWS + sh * 512 + 512],
                            start=(c == 0), stop=(c == NCH - 1)))(),
                            reads=[r_wgg[sl], r_xln], writes=[rg], tag="gmm")
                        P.op("tensor", (lambda c=c, jj=jj, sh=sh, pu=pu, sl=sl:
                             lambda e: e.matmul(
                            pu[:, :],
                            wgu[sl][:, c * QSUP + jj * 128:c * QSUP + (jj + 1) * 128],
                            xln[:, c * ROWS + sh * 512:c * ROWS + sh * 512 + 512],
                            start=(c == 0), stop=(c == NCH - 1)))(),
                            reads=[r_wgu[sl], r_xln], writes=[ru], tag="umm")
                    P.op("scalar", (lambda sh=sh, pg=pg, j=j: lambda e:
                         e.activation(
                        sgf[:, (j % 2) * 1024 + sh * 512:
                            (j % 2) * 1024 + sh * 512 + 512],
                        pg[:, :], AF.Silu))(),
                        reads=[rg], writes=[r_sgf], tag="silu")
                    P.op("vector", (lambda j=j, sh=sh, pu=pu: lambda e:
                         e.tensor_mul(
                        hbf[:, j * ROWS + sh * 512:j * ROWS + sh * 512 + 512],
                        sgf[:, (j % 2) * 1024 + sh * 512:
                            (j % 2) * 1024 + sh * 512 + 512], pu[:, :]))(),
                        reads=[r_sgf, ru], writes=[r_hbf], tag="hmul")
            qn = sup * 4 + quarter + 2
            if qn < nsup * 4:
                gate_dma(qn // 4, qn % 4)

        for eg in range(8):
            for ei in range(2):
                e_ch = eg * 2 + ei
                for sh in range(2):
                    pd, rd_ = ((p4, rp[4]), (p5, rp[5]),
                               (p6, rp[6]), (p7, rp[7]))[ei * 2 + sh]
                    for b in range(8):
                        P.op("tensor", (lambda e_ch=e_ch, sh=sh, b=b, pd=pd:
                             lambda e: e.matmul(
                            pd[:, :],
                            wdb[:, b * D + e_ch * 128:b * D + (e_ch + 1) * 128],
                            hbf[:, b * ROWS + sh * 512:b * ROWS + sh * 512 + 512],
                            start=(b == 0), stop=(b == 7)))(),
                            reads=[r_wdb, r_hbf], writes=[rd_], tag="dmm")
                    P.op("vector", (lambda e_ch=e_ch, sh=sh, pd=pd: lambda e:
                         e.tensor_add(
                        xres[:, e_ch * ROWS + sh * 512:e_ch * ROWS + sh * 512 + 512],
                        xres[:, e_ch * ROWS + sh * 512:e_ch * ROWS + sh * 512 + 512],
                        pd[:, :]))(),
                        reads=[rd_, r_xres], writes=[r_xres], tag="dacc")
        if sup + 1 < nsup:
            wd_dma(sup + 1)

    P.op("sync", lambda e: e.dma_start(
        out=outT.rearrange("(c p) s -> p c s", p=128),
        in_=xres[:, :].rearrange("p (c s) -> p c s", s=ROWS)),
        reads=[r_xres], writes=[r_outT], dma=True, tag="out_dma")
    P.op("gpsimd", lambda e: e.nop(), reads=[r_outT, r_scr], tag="final")

    P.emit(nc)
    es.close()
    return nc


def _shard_inputs(L0_attn_x, L0_attn_Wq, L0_attn_Wk, L0_attn_Wv, L0_attn_Wo,
                  L0_attn_betas, L0_ffn_w_gate_up, L0_ffn_w_down):
    x = np.ascontiguousarray(np.asarray(L0_attn_x, np.float32)[0])
    xT = np.ascontiguousarray(x.T)
    Wq = np.asarray(L0_attn_Wq, np.float32)
    Wk = np.asarray(L0_attn_Wk, np.float32)
    Wv = np.asarray(L0_attn_Wv, np.float32)
    Wo = np.ascontiguousarray(np.asarray(L0_attn_Wo, np.float32))
    betas = np.asarray(L0_attn_betas, np.float32)[0, :, 0, :]
    wgut = np.ascontiguousarray(np.asarray(L0_ffn_w_gate_up, np.float32).T)
    wdt = np.ascontiguousarray(np.asarray(L0_ffn_w_down, np.float32).T)

    in_maps = []
    row_idx = []
    for c in range(NC):
        rows = (np.arange(NSEG)[:, None] * SEG + 64 * c
                + np.arange(64)[None, :]).reshape(-1)
        row_idx.append(rows)
        in_maps.append({
            "xT": xT,
            "xrt": np.ascontiguousarray(x[rows].T),
            "wq": np.ascontiguousarray(Wq[:, c * HPC * DK:(c + 1) * HPC * DK]),
            "wk": np.ascontiguousarray(Wk[:, c * HPC * DK:(c + 1) * HPC * DK]),
            "wv": np.ascontiguousarray(Wv[:, c * HPC * DV:(c + 1) * HPC * DV]),
            "wo": Wo,
            "wgut": wgut,
            "wdt": wdt,
            "betasT": np.ascontiguousarray(betas[c * HPC:(c + 1) * HPC].T),
        })
    return in_maps, row_idx


def kernel(**inputs):
    in_maps, row_idx = _shard_inputs(**inputs)
    nc = build_program()
    res = run_bass_kernel_spmd(nc, in_maps, core_ids=list(range(NC)))
    out = np.empty((1, S, D), np.float32)
    for c in range(NC):
        out[0, row_idx[c]] = res.results[c]["outT"].T
    return out
